# version 17
# baseline (speedup 1.0000x reference)
import math
import numpy as np
import ml_dtypes
from contextlib import ExitStack

import concourse.bass as bass
import concourse.mybir as mybir
from concourse.bass_utils import run_bass_kernel_spmd

F32 = mybir.dt.float32
BF16 = mybir.dt.bfloat16
U32 = mybir.dt.uint32
AF = mybir.ActivationFunctionType
ALU = mybir.AluOpType
AX = mybir.AxisListType

H = 4096
S = 2048
NH = 32
HD = 128
NG = 16
GD = 256
GM = 1024
TOPK = 409
K1 = 8
BRACKET = 16.0
BHI = 6.0
BSPAN = BRACKET - BHI
QSCALE = 1.0 / math.sqrt(HD)
NMT = H // 128
ACT_TILES = 6
EPS = 1e-5


class Sem:
    __slots__ = ("h", "n")

    def __init__(self, h):
        self.h = h
        self.n = 0


class Plan:
    def __init__(self):
        self.ops = []

    def step(self, engine, build, waits=(), incs=(), war=False):
        wl = [(s.h, int(v)) for (s, v) in waits]
        il = [(s.h, int(k)) for (s, k) in incs]
        for (s, k) in incs:
            s.n += k

        def f(eng):
            for (hh, v) in wl:
                eng.wait_ge(hh, v)
            inst = build(eng)
            if il:
                assert inst is not None, "inc on step with no instruction"
                assert len(il) == 1, "one semaphore update per instruction"
                if engine in ("vector", "scalar") and not war:
                    eng.drain().then_inc(*il[0])
                else:
                    inst.then_inc(*il[0])
        self.ops.append((engine, f))

    def install(self, block):
        for name in ("sync", "vector", "scalar", "tensor", "gpsimd"):
            mine = [f for (e, f) in self.ops if e == name]
            if not mine:
                continue

            def body(eng, _mine=mine):
                for f in _mine:
                    f(eng)
            getattr(block, name)(body)


def build_program(nc, debug=False, upto=5):
    x0T = nc.dram_tensor("x0T", [H, S], F32, kind="ExternalInput").ap()
    wdev = {}
    for nm in ("wq", "wk", "wv", "wo"):
        wdev[nm] = nc.dram_tensor(nm + "_dev", [NMT, H, 128], F32, kind="ExternalInput").ap()
    bdev = {}
    for nm in ("bq", "bk", "bv", "bo"):
        bdev[nm] = nc.dram_tensor(nm + "_dev", [128, NMT], F32, kind="ExternalInput").ap()
    up_dev = nc.dram_tensor("up_dev", [NG, 128, 2 * GM], F32, kind="ExternalInput").ap()
    down_dev = nc.dram_tensor("down_dev", [NG, 128, 8 * GD], F32, kind="ExternalInput").ap()
    perm_dev = nc.dram_tensor("perm_dev", [128, 32], U32, kind="ExternalInput").ap()
    ident_b = nc.dram_tensor("ident_b", [128, 128], BF16, kind="ExternalInput").ap()
    ident_f = nc.dram_tensor("ident_f", [128, 128], F32, kind="ExternalInput").ap()
    OUT = nc.dram_tensor("OUT", [H, S], F32, kind="ExternalOutput").ap()

    skind = "ExternalOutput" if debug else "Internal"
    qT_d = nc.dram_tensor("qT_d", [H, S], BF16, kind=skind).ap()
    kT_d = nc.dram_tensor("kT_d", [H, S], BF16, kind=skind).ap()
    vT_d = nc.dram_tensor("vT_d", [H, S], BF16, kind=skind).ap()
    ctxT_d = nc.dram_tensor("ctxT_d", [H, S], BF16, kind=skind).ap()
    x1T_d = nc.dram_tensor("x1T_d", [H, S], F32, kind=skind).ap()
    yT_d = nc.dram_tensor("yT_d", [H, S], F32, kind=skind).ap()

    with ExitStack() as prog:
        sb = lambda name, shape, dt: prog.enter_context(nc.sbuf_tensor(name, shape, dt))
        sem = lambda name: Sem(prog.enter_context(nc.semaphore(name)))

        id_b = sb("id_b", [128, 128], BF16)
        id_f = sb("id_f", [128, 128], F32)
        ones_col = sb("ones_col", [128, 1], F32)
        ones_row = sb("ones_row", [1, 128], F32)
        c8 = sb("c8", [128, 8], F32)
        eps1 = sb("eps1", [1, 1], F32)
        f1 = sb("f1", [1, 1], F32)
        bias_sb = {nm: sb(nm + "_sb", [128, NMT], F32) for nm in ("bq", "bk", "bv", "bo")}
        perm_sb = sb("perm_sb", [128, 32], U32)

        s_init = sem("s_init")

        ln1scope = ExitStack()
        a1_row = ln1scope.enter_context(nc.sbuf_tensor("a1_row", [1, S], F32))
        b1_row = ln1scope.enter_context(nc.sbuf_tensor("b1_row", [1, S], F32))
        xln1 = ln1scope.enter_context(nc.sbuf_tensor("xln1", [128, NMT * S], BF16))
        def emit_ln_stats(tag, src_dram, a_row, b_row, first_steps=None):
            with ExitStack() as ph:
                pl = Plan()
                xbuf = [ph.enter_context(nc.sbuf_tensor(tag + "_x%d" % i, [128, S], F32))
                        for i in range(2)]
                sqbuf = [ph.enter_context(nc.sbuf_tensor(tag + "_sq%d" % i, [128, S], F32))
                         for i in range(2)]
                srow = ph.enter_context(nc.sbuf_tensor(tag + "_srow", [1, 2 * S], F32))
                ps_s1 = ph.enter_context(nc.psum_tensor(tag + "_s1", [1, S], F32))
                ps_s2 = ph.enter_context(nc.psum_tensor(tag + "_s2", [1, S], F32))
                ld = sem(tag + "_ld")
                sq = sem(tag + "_sq")
                pe = sem(tag + "_pe")
                dve = sem(tag + "_dve")
                act2 = sem(tag + "_act2")
                if first_steps is not None:
                    first_steps(pl, ld)
                for c in range(NMT):
                    bi = c % 2
                    waits = [(pe, (c - 1))] if c >= 2 else []
                    pl.step("sync", lambda e, _c=c, _b=bi: e.dma_start(
                        out=xbuf[_b][:], in_=src_dram[_c * 128:(_c + 1) * 128, :]),
                        waits=waits, incs=[(ld, 16)])
                    pl.step("scalar", lambda e, _b=bi: e.activation(
                        out=sqbuf[_b][:], in_=xbuf[_b][:], func=AF.Square),
                        waits=[(ld, ld.n)], incs=[(sq, 1)], war=True)

                    def mm_s(e, _c=c, _b=bi):
                        last = None
                        for n in range(4):
                            last = e.matmul(ps_s1[0:1, n * 512:(n + 1) * 512],
                                            lhsT=ones_col[:],
                                            rhs=xbuf[_b][:, n * 512:(n + 1) * 512],
                                            start=(_c == 0), stop=(_c == NMT - 1))
                        for n in range(4):
                            last = e.matmul(ps_s2[0:1, n * 512:(n + 1) * 512],
                                            lhsT=ones_col[:],
                                            rhs=sqbuf[_b][:, n * 512:(n + 1) * 512],
                                            start=(_c == 0), stop=(_c == NMT - 1))
                        return last
                    pl.step("tensor", mm_s,
                            waits=[(ld, ld.n), (sq, sq.n), (s_init, 1)],
                            incs=[(pe, 1)])

                mu = srow[0:1, 0:S]
                ms = srow[0:1, S:2 * S]
                var = sqbuf[0][0:1, 0:S]

                def stats_evac(v):
                    v.tensor_scalar(out=mu, in0=ps_s1[:], scalar1=1.0 / H,
                                    scalar2=None, op0=ALU.mult)
                    v.tensor_scalar(out=ms, in0=ps_s2[:], scalar1=1.0 / H,
                                    scalar2=None, op0=ALU.mult)
                    v.drain()
                    v.tensor_tensor(out=b_row[:], in0=mu, in1=mu, op=ALU.mult)
                    v.drain()
                    return v.tensor_tensor(out=var, in0=ms, in1=b_row[:], op=ALU.subtract)
                pl.step("vector", stats_evac, waits=[(pe, pe.n)], incs=[(dve, 1)])

                def rstd_act(e):
                    e.activation(out=b_row[:], in_=var, func=AF.Ln, bias=eps1[:])
                    e.drain()
                    return e.activation(out=a_row[:], in_=b_row[:], func=AF.Exp, scale=-0.5)
                pl.step("scalar", rstd_act, waits=[(dve, dve.n)], incs=[(act2, 1)])

                def b_calc(v):
                    v.tensor_tensor(out=b_row[:], in0=mu, in1=a_row[:], op=ALU.mult)
                    v.drain()
                    return v.tensor_scalar(out=b_row[:], in0=b_row[:], scalar1=-1.0,
                                           scalar2=None, op0=ALU.mult)
                pl.step("vector", b_calc, waits=[(act2, act2.n)], incs=[(dve, 1)])

                block = ph.enter_context(nc.Block())
                pl.install(block)

        def p0_first_steps(pl, ld):
            def init_consts(v):
                v.memset(ones_col[:], 1.0)
                v.memset(ones_row[:], 1.0)
                v.memset(eps1[:], EPS)
                v.memset(f1[:], 0.0)
                last = None
                for j in range(8):
                    last = v.memset(c8[:, j:j + 1], float(j))
                return last
            pl.step("vector", init_consts, incs=[(s_init, 1)])
            pl.step("sync", lambda e: e.dma_start(out=id_b[:], in_=ident_b), incs=[(ld, 16)])
            pl.step("sync", lambda e: e.dma_start(out=id_f[:], in_=ident_f), incs=[(ld, 16)])
            pl.step("sync", lambda e: e.dma_start(out=perm_sb[:], in_=perm_dev), incs=[(ld, 16)])
            for nm in ("bq", "bk", "bv", "bo"):
                pl.step("sync", lambda e, _n=nm: e.dma_start(out=bias_sb[_n][:], in_=bdev[_n]),
                        incs=[(ld, 16)])

        emit_ln_stats("p0", x0T, a1_row, b1_row, first_steps=p0_first_steps)

        if True:
            with ExitStack() as ph:
                pl = Plan()
                xbuf = [ph.enter_context(nc.sbuf_tensor("p0b_x%d" % i, [128, S], F32)) for i in range(2)]
                tmpb = [ph.enter_context(nc.sbuf_tensor("p0b_t%d" % i, [128, S], F32)) for i in range(2)]
                a1b = ph.enter_context(nc.sbuf_tensor("a1b", [128, S], F32))
                b1b = ph.enter_context(nc.sbuf_tensor("b1b", [128, S], F32))
                ps_a = ph.enter_context(nc.psum_tensor("p0b_a", [128, S], F32))
                ps_b = ph.enter_context(nc.psum_tensor("p0b_b", [128, S], F32))
                ld = sem("p0b_ld")
                pe = sem("p0b_pe")
                dve = sem("p0b_dve")
                nrm = sem("p0b_nrm")

                def bc_mm(e):
                    last = None
                    for n in range(4):
                        last = e.matmul(ps_a[:, n * 512:(n + 1) * 512], lhsT=ones_row[:],
                                        rhs=a1_row[0:1, n * 512:(n + 1) * 512], start=True, stop=True)
                    for n in range(4):
                        last = e.matmul(ps_b[:, n * 512:(n + 1) * 512], lhsT=ones_row[:],
                                        rhs=b1_row[0:1, n * 512:(n + 1) * 512], start=True, stop=True)
                    return last
                pl.step("tensor", bc_mm, incs=[(pe, 1)])

                def bc_evac(v):
                    v.tensor_copy(a1b[:], ps_a[:])
                    return v.tensor_copy(b1b[:], ps_b[:])
                pl.step("vector", bc_evac, waits=[(pe, 1)], incs=[(dve, 1)])

                for c in range(NMT):
                    bi = c % 2
                    waits = [(nrm, (c - 1))] if c >= 2 else []
                    pl.step("sync", lambda e, _c=c, _b=bi: e.dma_start(
                        out=xbuf[_b][:], in_=x0T[_c * 128:(_c + 1) * 128, :]),
                        waits=waits, incs=[(ld, 16)])

                    def norm(v, _c=c, _b=bi):
                        v.tensor_tensor(out=tmpb[_b][:], in0=xbuf[_b][:], in1=a1b[:], op=ALU.mult)
                        v.drain()
                        return v.tensor_tensor(out=xln1[:, _c * S:(_c + 1) * S],
                                               in0=tmpb[_b][:], in1=b1b[:], op=ALU.add)
                    pl.step("vector", norm, waits=[(ld, ld.n), (dve, 1)],
                            incs=[(nrm, 1)])

                block = ph.enter_context(nc.Block())
                pl.install(block)

            if upto >= 1:
              with ExitStack() as ph:
                pl = Plan()
                wb = [ph.enter_context(nc.sbuf_tensor("p1_wb%d" % i, [128, H], BF16)) for i in range(2)]
                otile = [ph.enter_context(nc.sbuf_tensor("p1_o%d" % i, [128, S], BF16)) for i in range(2)]
                ps_o = [ph.enter_context(nc.psum_tensor("p1_ps%d" % i, [128, S], F32)) for i in range(2)]
                ld = sem("p1_ld")
                cast = sem("p1_cast")
                mmd = sem("p1_mm")
                ev = sem("p1_ev")
                st = sem("p1_st")

                mats = [("wq", "bq", qT_d), ("wk", "bk", kT_d), ("wv", "bv", vT_d)]
                k = 0
                for (wn, bn, dst) in mats:
                    for mt in range(NMT):
                        bi = k % 2
                        waits = [(mmd, k - 1)] if k >= 2 else []
                        pl.step("gpsimd", lambda e, _w=wn, _mt=mt, _b=bi: e.dma_start(
                            out=wb[_b][:],
                            in_=wdev[_w][_mt].rearrange("(c p) o -> c p o", p=128).transpose([1, 0, 2])),
                            waits=waits, incs=[(ld, 16)])
                        waits = [(ld, ld.n)]
                        if k >= 2:
                            waits.append((ev, k - 1))

                        def mm(e, _b=bi):
                            last = None
                            for n in range(4):
                                for c in range(NMT):
                                    last = e.matmul(
                                        ps_o[_b][:, n * 512:(n + 1) * 512],
                                        lhsT=wb[_b][:, c * 128:(c + 1) * 128],
                                        rhs=xln1[:, c * S + n * 512: c * S + (n + 1) * 512],
                                        start=(c == 0), stop=(c == NMT - 1))
                            return last
                        pl.step("tensor", mm, waits=waits, incs=[(mmd, 1)])
                        waits = [(mmd, mmd.n)]
                        if k >= 2:
                            waits.append((st, 16 * (k - 1)))
                        pl.step("vector", lambda v, _b=bi, _bn=bn, _mt=mt: v.tensor_scalar(
                            out=otile[_b][:], in0=ps_o[_b][:],
                            scalar1=bias_sb[_bn][:, _mt:_mt + 1], scalar2=None, op0=ALU.add),
                            waits=waits, incs=[(ev, 1)])
                        pl.step("sync", lambda e, _b=bi, _d=dst, _mt=mt: e.dma_start(
                            out=_d[_mt * 128:(_mt + 1) * 128, :], in_=otile[_b][:]),
                            waits=[(ev, ev.n)], incs=[(st, 16)])
                        k += 1
                pl.step("sync", lambda e: e.nop(), waits=[(st, st.n)])
                block = ph.enter_context(nc.Block())
                pl.install(block)
        if debug and upto == 0:
            with ExitStack() as ph:
                pl = Plan()
                dst = sem("dbg0_st")
                for c in range(NMT):
                    pl.step("sync", lambda e, _c=c: e.dma_start(
                        out=qT_d[_c * 128:(_c + 1) * 128, :],
                        in_=xln1[:, _c * S:(_c + 1) * S]), incs=[(dst, 16)])
                pl.step("sync", lambda e: e.nop(), waits=[(dst, dst.n)])
                block = ph.enter_context(nc.Block())
                pl.install(block)
        ln1scope.close()
        if upto < 2:
            return nc

        with ExitStack() as ph:
            pl = Plan()
            ET = [ph.enter_context(nc.sbuf_tensor("p2_E%d" % i, [128, 8 * S], F32)) for i in range(2)]
            qh = ph.enter_context(nc.sbuf_tensor("p2_q", [128, S], BF16))
            kh = ph.enter_context(nc.sbuf_tensor("p2_k", [128, S], BF16))
            vh = ph.enter_context(nc.sbuf_tensor("p2_v", [128, S], BF16))
            Vt = [ph.enter_context(nc.sbuf_tensor("p2_V%d" % i, [128, S], BF16)) for i in range(2)]
            sc1 = ph.enter_context(nc.sbuf_tensor("p2_sc1", [128, S], F32))
            sc2 = ph.enter_context(nc.sbuf_tensor("p2_sc2", [128, S], F32))
            sc_act = ph.enter_context(nc.sbuf_tensor("p2_scact", [128, S], BF16))
            Pbf = [ph.enter_context(nc.sbuf_tensor("p2_P%d" % i, [128, S], BF16)) for i in range(2)]
            PT4 = ph.enter_context(nc.sbuf_tensor("p2_PT4", [128, 16 * 512], BF16))
            ctxh = ph.enter_context(nc.sbuf_tensor("p2_ctxh", [128, S], BF16))
            zb_sb = ph.enter_context(nc.sbuf_tensor("p2_zb", [128, 512], F32))
            zrow = ph.enter_context(nc.sbuf_tensor("p2_zrow", [1, 512], F32))
            top8 = ph.enter_context(nc.sbuf_tensor("p2_top8", [128, 8], F32))
            t8t = ph.enter_context(nc.sbuf_tensor("p2_t8t", [128, 8], F32))

            def stt(i, name):
                return ph.enter_context(nc.sbuf_tensor("p2_%s%d" % (name, i), [128, 8], F32))
            st_zsum = [stt(i, "zsum") for i in range(2)]
            st_tlo = [[stt(i, "tlo%d" % j) for i in range(2)] for j in range(2)]
            st_thi = [[stt(i, "thi%d" % j) for i in range(2)] for j in range(2)]
            st_chi = [[stt(i, "chi%d" % j) for i in range(2)] for j in range(2)]
            st_cnt = [stt(i, "cnt") for i in range(2)]
            st_zsum2 = [ph.enter_context(nc.sbuf_tensor("p2_zs2_%d" % i, [128, 16], F32))
                        for i in range(2)]
            st_tmid = [stt(i, "tmid") for i in range(2)]
            st_tm2 = [stt(i, "tm2") for i in range(2)]
            st_ge = [ph.enter_context(nc.sbuf_tensor("p2_ge%d" % i, [128, 8], U32)) for i in range(2)]
            st_gn = [ph.enter_context(nc.sbuf_tensor("p2_gn%d" % i, [128, 8], U32)) for i in range(2)]
            st_u2s = [stt(i, "u2s") for i in range(2)]
            st_u = [stt(i, "u") for i in range(2)]
            st_ulo = [stt(i, "ulo") for i in range(2)]
            st_uhi = [stt(i, "uhi") for i in range(2)]
            st_rm1 = [stt(i, "rm1") for i in range(2)]
            st_tst = [stt(i, "tst") for i in range(2)]
            st_tst2 = [stt(i, "tst2") for i in range(2)]
            st_zsel = [stt(i, "zsel") for i in range(2)]
            st_zinv = [stt(i, "zinv") for i in range(2)]

            ps_s = ph.enter_context(nc.psum_tensor("p2_psS", [128, S], F32))
            ps_pt = ph.enter_context(nc.psum_tensor("p2_pspt", [128, 1024], BF16))
            ps_cx = [ph.enter_context(nc.psum_tensor("p2_pscx%d" % i, [128, 512], F32)) for i in range(2)]
            ps_aux = ph.enter_context(nc.psum_tensor("p2_psaux", [128, 512], F32))

            s_ld = sem("p2_ld")
            s_vtr = sem("p2_vtr")
            s_vev = sem("p2_vev")
            s_smm = sem("p2_smm")
            s_eev = sem("p2_eev")
            s_tmid = sem("p2_tmid")
            s_u = sem("p2_u")
            s_acts = sem("p2_acts")
            s_fin = sem("p2_fin")
            s_ptt = sem("p2_ptt")
            s_ptev = sem("p2_ptev")
            s_pv = sem("p2_pv")
            s_trc = sem("p2_trc")
            s_zinv = sem("p2_zinv")
            s_zrow = sem("p2_zrow")
            s_zbm = sem("p2_zbm")
            s_zbev = sem("p2_zbev")
            s_ctx = sem("p2_ctx")
            s_cst = sem("p2_cst")

            NU = NH * 2
            ld_head = {}

            def dma_head(h):
                waits = []
                if h >= 1:
                    waits = [(s_smm, 32 * h), (s_vtr, 2 * h)]
                for (dst, src) in ((qh, qT_d), (kh, kT_d), (vh, vT_d)):
                    pl.step("sync", lambda e, _d=dst, _s=src, _h=h: e.dma_start(
                        out=_d[:], in_=_s[_h * 128:(_h + 1) * 128, :]),
                        waits=waits, incs=[(s_ld, 16)])
                    waits = []
                ld_head[h] = s_ld.n

            def emit_vtr(h):
                par = h % 2
                for half in range(2):
                    def tr(e, _hf=half):
                        last = None
                        for t in range(8):
                            kc = _hf * 8 + t
                            last = e.transpose(ps_pt[:, t * 128:(t + 1) * 128],
                                               vh[:, kc * 128:(kc + 1) * 128], id_b[:])
                        return last
                    pl.step("tensor", tr,
                            waits=[(s_ld, ld_head[h]), (s_ptev, s_ptev.n)],
                            incs=[(s_vtr, 1)])
                    pl.step("vector", lambda v, _p=par, _hf=half: v.tensor_copy(
                        Vt[_p][:, _hf * 1024:(_hf + 1) * 1024], ps_pt[:]),
                        waits=[(s_vtr, s_vtr.n)], incs=[(s_ptev, 1)], war=True)

            def emit_stage_S(u):
                h, hf = u // 2, u % 2
                par = u % 2
                for j in range(8):
                    qt = hf * 8 + j
                    for half in range(2):
                        def smm(e, _h=h, _qt=qt, _hf=half):
                            last = None
                            for n in range(2 * _hf, 2 * _hf + 2):
                                last = e.matmul(ps_s[:, n * 512:(n + 1) * 512],
                                                lhsT=qh[:, _qt * 128:(_qt + 1) * 128],
                                                rhs=kh[:, n * 512:(n + 1) * 512],
                                                start=True, stop=True)
                            return last
                        pl.step("tensor", smm,
                                waits=[(s_ld, ld_head[h]),
                                       (s_eev, max(0, s_eev.n - 1))],
                                incs=[(s_smm, 1)])
                        waits = [(s_smm, s_smm.n)]
                        if u >= 2 and half == 0:
                            waits.append((s_fin, 8 * (u - 2) + j + 1))
                        pl.step("scalar", lambda e, _par=par, _j=j, _hf=half: e.activation(
                            out=ET[_par][:, _j * S + _hf * 1024:_j * S + _hf * 1024 + 1024],
                            in_=ps_s[:, _hf * 1024:(_hf + 1) * 1024], func=AF.Exp,
                            scale=QSCALE,
                            accum_out=st_zsum2[_par][:, 2 * _j + _hf:2 * _j + _hf + 1]),
                            waits=waits, incs=[(s_eev, 1)],
                            war=not (j == 7 and half == 1))

            def emit_bisect(u, fill=()):
                fill = list(fill)
                total = sum(c for c, _ in fill)
                budget = total / K1
                spent = 0.0
                par = u % 2
                uu, hi, ch = st_tlo[0][par], st_thi[0][par], st_chi[0][par]
                uu2, hi2, ch2 = st_tlo[1][par], st_thi[1][par], st_chi[1][par]
                while fill and spent < total * 0.25:
                    c, fn = fill.pop(0)
                    fn()
                    spent += c
                spent = 0.0
                budget = sum(c for c, _ in fill) / K1

                def init_d(v, _p=par, _u=uu, _hi=hi, _ch=ch):
                    v.drain()
                    z2 = st_zsum2[_p][:].rearrange("p (c t) -> p c t", t=2)
                    z1 = st_zsum[_p][:].rearrange("p (c t) -> p c t", t=1)
                    v.tensor_tensor(out=z1, in0=z2[:, :, 0:1], in1=z2[:, :, 1:2],
                                    op=ALU.add)
                    v.drain()
                    v.tensor_scalar(out=_u[:], in0=st_zsum[_p][:],
                                    scalar1=math.exp(-(BRACKET + BHI) / 2),
                                    scalar2=None, op0=ALU.mult)
                    return v.memset(_ch[:], 0.0)
                pl.step("vector", init_d, waits=[(s_eev, s_eev.n)], incs=[(s_u, 1)])

                for it in range(K1):
                    w = (BSPAN / 4) / (2 ** it)
                    EW = math.exp(-w)
                    DW = math.exp(w) - EW
                    for j in range(ACT_TILES):
                        waits = [(s_u, s_u.n)] if j == 0 else []
                        incs = [(s_acts, 1)] if j == ACT_TILES - 1 else []
                        pl.step("scalar", lambda e, _p=par, _j=j, _u=uu: e.activation(
                            out=sc_act[:], in_=ET[_p][:, _j * S:(_j + 1) * S], func=AF.Sign,
                            scale=-1.0, bias=_u[:, _j:_j + 1],
                            accum_out=st_cnt[_p][:, _j:_j + 1]),
                            waits=waits, incs=incs, war=True)
                    for j in range(ACT_TILES, 8):
                        pl.step("vector", lambda v, _p=par, _j=j, _u=uu: v.tensor_scalar(
                            out=sc2[:], in0=ET[_p][:, _j * S:(_j + 1) * S],
                            scalar1=_u[:, _j:_j + 1], scalar2=None,
                            op0=ALU.is_ge, op1=ALU.add,
                            accum_out=st_cnt[_p][:, _j:_j + 1]))

                    while fill and spent < budget * (it + 1):
                        c, fn = fill.pop(0)
                        fn()
                        spent += c

                    def upd_crit(v, _p=par, _u=uu, _u2=uu2, _EW=EW, _DW=DW):
                        v.tensor_scalar(out=st_u[_p][:, 0:ACT_TILES],
                                        in0=st_cnt[_p][:, 0:ACT_TILES],
                                        scalar1=float(2048 - 2 * TOPK), scalar2=_DW,
                                        op0=ALU.is_le, op1=ALU.mult)
                        v.tensor_scalar(out=st_u[_p][:, ACT_TILES:8],
                                        in0=st_cnt[_p][:, ACT_TILES:8],
                                        scalar1=float(TOPK), scalar2=_DW,
                                        op0=ALU.is_ge, op1=ALU.mult)
                        v.drain()
                        return v.scalar_tensor_tensor(
                            out=_u2[:], in0=st_u[_p][:], scalar=_EW,
                            in1=_u[:], op0=ALU.add, op1=ALU.mult)
                    pl.step("vector", upd_crit, waits=[(s_acts, s_acts.n)],
                            incs=[(s_u, 1)])

                    last = (it == K1 - 1)
                    def upd_rest(v, _p=par, _u=uu, _u2=uu2, _hi=hi, _ch=ch,
                                 _hi2=hi2, _ch2=ch2, _last=last):
                        v.tensor_scalar(out=st_tm2[_p][:, 0:ACT_TILES],
                                        in0=st_cnt[_p][:, 0:ACT_TILES],
                                        scalar1=-0.5, scalar2=1024.0,
                                        op0=ALU.mult, op1=ALU.add)
                        v.tensor_scalar(out=st_tm2[_p][:, ACT_TILES:8],
                                        in0=st_cnt[_p][:, ACT_TILES:8],
                                        scalar1=1.0, scalar2=None, op0=ALU.mult)
                        v.tensor_scalar(out=st_ge[_p][:, 0:ACT_TILES],
                                        in0=st_cnt[_p][:, 0:ACT_TILES],
                                        scalar1=float(2048 - 2 * TOPK), scalar2=None,
                                        op0=ALU.is_le)
                        v.tensor_scalar(out=st_ge[_p][:, ACT_TILES:8],
                                        in0=st_cnt[_p][:, ACT_TILES:8],
                                        scalar1=float(TOPK), scalar2=None, op0=ALU.is_ge)
                        r = v.select(_ch2[:], st_ge[_p][:], _ch[:],
                                     st_tm2[_p][:], add_drain=True)
                        if _last:
                            r = v.tensor_scalar(
                                out=st_ulo[_p][:], in0=_u2[:],
                                scalar1=math.exp(-(BSPAN / 2) / (2 ** K1)),
                                scalar2=None, op0=ALU.mult)
                            r = v.tensor_scalar(
                                out=st_uhi[_p][:], in0=_u2[:],
                                scalar1=math.exp((BSPAN / 2) / (2 ** K1)),
                                scalar2=None, op0=ALU.mult)
                        return r
                    pl.step("vector", upd_rest,
                            incs=([(s_tmid, 1)] if last else []))
                    uu, uu2 = uu2, uu
                    ch, ch2 = ch2, ch
                for _c, fn in fill:
                    fn()
                return uu, st_uhi[par], ch

            def make_tail_thunks(u, ch, hi):
                thunks = []
                par = u % 2
                def rm1_f(v, _p=par, _ch=ch):
                    v.drain()
                    v.tensor_scalar(out=st_rm1[_p][:], in0=_ch[:], scalar1=-1.0,
                                    scalar2=408.0, op0=ALU.mult, op1=ALU.add)
                    return v.drain()
                thunks.append((2.3, lambda _f=rm1_f, _w=s_tmid.n: pl.step(
                    "vector", _f, waits=[(s_tmid, _w)])))
                for j in range(8):
                    def r2(v, _p=par, _j=j, _hi=hi):
                        v.scalar_tensor_tensor(
                            out=sc2[:], in0=ET[_p][:, _j * S:(_j + 1) * S],
                            scalar=_hi[:, _j:_j + 1],
                            in1=ET[_p][:, _j * S:(_j + 1) * S],
                            op0=ALU.is_lt, op1=ALU.mult)
                        v.drain()
                        v.max(out=top8[:], in_=sc2[:])
                        v.drain()
                        return v.scalar_tensor_tensor(
                            out=t8t[:], in0=c8[:], scalar=st_rm1[_p][:, _j:_j + 1],
                            in1=top8[:], op0=ALU.is_equal, op1=ALU.mult,
                            accum_out=st_tst[_p][:, _j:_j + 1])
                    thunks.append((2.3, lambda _f=r2: pl.step("vector", _f)))
                def tstmax_f(v, _p=par):
                    v.drain()
                    v.tensor_tensor(out=st_tst2[_p][:], in0=st_tst[_p][:],
                                    in1=st_ulo[_p][:], op=ALU.max)
                    return v.drain()
                thunks.append((0.4, lambda _f=tstmax_f: pl.step("vector", _f)))

                h, hf = u // 2, u % 2
                hpar = h % 2
                for j in range(8):
                    def fin_f(v, _p=par, _j=j):
                        return v.scalar_tensor_tensor(
                            out=Pbf[_j % 2][:], in0=ET[_p][:, _j * S:(_j + 1) * S],
                            scalar=st_tst2[_p][:, _j:_j + 1],
                            in1=ET[_p][:, _j * S:(_j + 1) * S],
                            op0=ALU.is_ge, op1=ALU.mult,
                            accum_out=st_zsel[_p][:, _j:_j + 1])
                    thunks.append((2.3, lambda _f=fin_f: pl.step(
                        "vector", _f, waits=[(s_ptt, s_ptt.n)],
                        incs=[(s_fin, 1)], war=True)))
                    for half in range(2):
                        def ptr(e, _j=j, _hf=half):
                            last = None
                            for t in range(8):
                                kc = _hf * 8 + t
                                last = e.transpose(ps_pt[:, t * 128:(t + 1) * 128],
                                                   Pbf[_j % 2][:, kc * 128:(kc + 1) * 128],
                                                   id_b[:])
                            return last
                        thunks.append((0.0, lambda _f=ptr: pl.step(
                            "tensor", _f,
                            waits=[(s_fin, s_fin.n), (s_ptev, s_ptev.n)],
                            incs=[(s_ptt, 1)])))

                        def ptev(v, _j=j, _hf=half):
                            dst = PT4[:].rearrange("p (kc q) -> p kc q", kc=16)
                            dst = dst[:, _hf * 8:(_hf + 1) * 8,
                                      (_j % 4) * 128:(_j % 4) * 128 + 128]
                            src = ps_pt[:].rearrange("p (kc q) -> p kc q", kc=8)
                            return v.tensor_copy(dst, src)
                        thunks.append((0.7, lambda _f=ptev: pl.step(
                            "vector", _f,
                            waits=[(s_ptt, s_ptt.n), (s_pv, s_pv.n)],
                            incs=[(s_ptev, 1)], war=True)))
                    if j % 4 == 3:
                        g = j // 4
                        cpar = g % 2

                        def pv(e, _hp=hpar, _c=cpar):
                            last = None
                            for kc in range(16):
                                last = e.matmul(ps_cx[_c][:],
                                                lhsT=Vt[_hp][:, kc * 128:(kc + 1) * 128],
                                                rhs=PT4[:, kc * 512:(kc + 1) * 512],
                                                start=(kc == 0), stop=(kc == 15))
                            return last
                        thunks.append((0.0, lambda _f=pv: pl.step(
                            "tensor", _f,
                            waits=[(s_ptev, s_ptev.n), (s_vtr, s_vtr.n), (s_ctx, s_ctx.n)],
                            incs=[(s_pv, 1)])))
                        def zi_f(v, _p=par, _g=g):
                            return v.reciprocal(st_zinv[_p][:, _g * 4:_g * 4 + 4],
                                                st_zsel[_p][:, _g * 4:_g * 4 + 4])
                        thunks.append((0.4, lambda _f=zi_f: pl.step(
                            "vector", _f, incs=[(s_zinv, 1)])))

                        def trc(e, _p=par, _g=g):
                            last = None
                            for t in range(4):
                                last = e.transpose(ps_aux[0:1, t * 128:(t + 1) * 128],
                                                   st_zinv[_p][:, _g * 4 + t:_g * 4 + t + 1],
                                                   id_f[:])
                            return last
                        thunks.append((0.0, lambda _f=trc: pl.step(
                            "tensor", _f,
                            waits=[(s_zbev, s_zbev.n), (s_zinv, s_zinv.n)],
                            incs=[(s_trc, 1)])))
                        thunks.append((0.4, lambda: pl.step(
                            "vector", lambda v: v.tensor_copy(zrow[:], ps_aux[0:1, :]),
                            waits=[(s_trc, s_trc.n)], incs=[(s_zrow, 1)], war=True)))

                        def zbm(e):
                            last = None
                            for t in range(4):
                                last = e.matmul(ps_aux[:, t * 128:(t + 1) * 128],
                                                lhsT=ones_row[:],
                                                rhs=zrow[0:1, t * 128:(t + 1) * 128],
                                                start=True, stop=True)
                            return last
                        thunks.append((0.0, lambda _f=zbm: pl.step(
                            "tensor", _f, waits=[(s_zrow, s_zrow.n)], incs=[(s_zbm, 1)])))
                        thunks.append((0.7, lambda: pl.step(
                            "vector", lambda v: v.tensor_copy(zb_sb[:], ps_aux[:]),
                            waits=[(s_zbm, s_zbm.n)], incs=[(s_zbev, 1)], war=True)))

                        def ctxev(v, _hf=hf, _g=g, _c=cpar):
                            v.drain()
                            dst = ctxh[:, (_hf * 8 + _g * 4) * 128:(_hf * 8 + _g * 4) * 128 + 512]
                            return v.tensor_tensor(out=dst, in0=ps_cx[_c][:], in1=zb_sb[:],
                                                   op=ALU.mult)
                        thunks.append((0.7, lambda _f=ctxev: pl.step(
                            "vector", _f,
                            waits=[(s_pv, s_pv.n), (s_zbev, s_zbev.n), (s_cst, s_cst.n)],
                            incs=[(s_ctx, 1)], war=True)))
                if hf == 1:
                    def cst_f(e, _h=h):
                        return e.dma_start(out=ctxT_d[_h * 128:(_h + 1) * 128, :],
                                           in_=ctxh[:])
                    thunks.append((0.0, lambda _f=cst_f: pl.step(
                        "sync", _f, waits=[(s_ctx, s_ctx.n)], incs=[(s_cst, 16)])))
                return thunks

            dma_head(0)
            emit_vtr(0)
            emit_stage_S(0)
            pending = []
            for u in range(NU):
                h, hf = u // 2, u % 2
                if hf == 0 and h + 1 < NH:
                    dma_head(h + 1)
                lo, hi, ch = emit_bisect(u, pending)
                if u + 1 < NU:
                    if (u + 1) % 2 == 0:
                        emit_vtr((u + 1) // 2)
                    emit_stage_S(u + 1)
                pending = make_tail_thunks(u, ch, hi)
            for _c, fn in pending:
                fn()
            pl.step("sync", lambda e: e.nop(), waits=[(s_cst, s_cst.n)])
            if debug and upto == 2:
                dbg = sem("p2_dbg")
                state_list = [st_zsum[1], st_tlo[0][1], st_tlo[1][1], st_thi[0][1],
                              st_thi[1][1], st_chi[0][1], st_chi[1][1], st_cnt[1],
                              st_u[1], st_ulo[1], st_uhi[1], st_tst[1], st_tst2[1],
                              st_zsel[1], st_zinv[1], st_tmid[1]]
                for si, stn in enumerate(state_list):
                    pl.step("sync", lambda e, _s=stn, _i=si: e.dma_start(
                        out=yT_d[0:128, _i * 8:(_i + 1) * 8], in_=_s[:]),
                        incs=[(dbg, 16)])
                for j in range(8):
                    pl.step("sync", lambda e, _j=j: e.dma_start(
                        out=yT_d[128 * (1 + _j):128 * (2 + _j), :],
                        in_=ET[1][:, _j * S:(_j + 1) * S]), incs=[(dbg, 16)])
                pl.step("sync", lambda e: e.nop(), waits=[(dbg, dbg.n)])
            block = ph.enter_context(nc.Block())
            pl.install(block)

        if upto < 3:
            return nc

        ln2scope = ExitStack()
        a2_row = ln2scope.enter_context(nc.sbuf_tensor("a2_row", [1, S], F32))
        b2_row = ln2scope.enter_context(nc.sbuf_tensor("b2_row", [1, S], F32))
        with ExitStack() as ph:
            pl = Plan()
            ctxsb = ph.enter_context(nc.sbuf_tensor("p3_ctx", [128, NMT * S], BF16))
            wb = ph.enter_context(nc.sbuf_tensor("p3_wb", [128, H], BF16))
            x0t = [ph.enter_context(nc.sbuf_tensor("p3_x0%d" % i, [128, S], F32)) for i in range(2)]
            x1t = [ph.enter_context(nc.sbuf_tensor("p3_x1%d" % i, [128, S], F32)) for i in range(2)]
            ps_o = ph.enter_context(nc.psum_tensor("p3_ps", [128, S], F32))
            ld = sem("p3_ld")
            ldx = sem("p3_ldx")
            ldc = sem("p3_ldc")
            mmd = sem("p3_mm")
            ev = sem("p3_ev")
            st = sem("p3_st")

            for c in range(NMT):
                pl.step("sync", lambda e, _c=c: e.dma_start(
                    out=ctxsb[:, _c * S:(_c + 1) * S],
                    in_=ctxT_d[_c * 128:(_c + 1) * 128, :]), incs=[(ldc, 16)])

            for mt in range(NMT):
                bi = mt % 2
                waits = [(mmd, mt)] if mt >= 1 else []
                pl.step("gpsimd", lambda e, _mt=mt: e.dma_start(
                    out=wb[:],
                    in_=wdev["wo"][_mt].rearrange("(c p) o -> c p o", p=128).transpose([1, 0, 2])),
                    waits=waits, incs=[(ld, 16)])
                waits = [(ev, mt - 1)] if mt >= 2 else []
                pl.step("sync", lambda e, _mt=mt, _b=bi: e.dma_start(
                    out=x0t[_b][:], in_=x0T[_mt * 128:(_mt + 1) * 128, :]),
                    waits=waits, incs=[(ldx, 16)])
                waits = [(ld, ld.n), (ldx, ldx.n), (ldc, ldc.n)]
                if mt >= 1:
                    waits.append((ev, mt))

                def mm(e):
                    last = None
                    for n in range(4):
                        for c in range(NMT):
                            last = e.matmul(
                                ps_o[:, n * 512:(n + 1) * 512],
                                lhsT=wb[:, c * 128:(c + 1) * 128],
                                rhs=ctxsb[:, c * S + n * 512: c * S + (n + 1) * 512],
                                start=(c == 0), stop=(c == NMT - 1))
                    return last
                pl.step("tensor", mm, waits=waits, incs=[(mmd, 1)])
                waits = [(mmd, mmd.n), (ld, ld.n)]
                if mt >= 2:
                    waits.append((st, 16 * (mt - 1)))
                pl.step("vector", lambda v, _b=bi, _mt=mt: v.scalar_tensor_tensor(
                    out=x1t[_b][:], in0=ps_o[:], scalar=bias_sb["bo"][:, _mt:_mt + 1],
                    in1=x0t[_b][:], op0=ALU.add, op1=ALU.add),
                    waits=waits, incs=[(ev, 1)])
                pl.step("sync", lambda e, _b=bi, _mt=mt: e.dma_start(
                    out=OUT[_mt * 128:(_mt + 1) * 128, :], in_=x1t[_b][:]),
                    waits=[(ev, ev.n)], incs=[(st, 16)])
            pl.step("sync", lambda e: e.nop(), waits=[(st, st.n)])

            block = ph.enter_context(nc.Block())
            pl.install(block)

        emit_ln_stats("p3b", OUT, a2_row, b2_row)

        if upto < 4:
            ln2scope.close()
            return nc
        with ExitStack() as ph:
            pl = Plan()
            a2b = ph.enter_context(nc.sbuf_tensor("p4_a2b", [128, S], F32))
            b2b = ph.enter_context(nc.sbuf_tensor("p4_b2b", [128, S], F32))
            gbuf = [ph.enter_context(nc.sbuf_tensor("p4_g%d" % i, [128, S], F32)) for i in range(4)]
            tgb = [ph.enter_context(nc.sbuf_tensor("p4_tg%d" % i, [128, S], BF16)) for i in range(4)]
            upb = [ph.enter_context(nc.sbuf_tensor("p4_upb%d" % i, [128, 2 * GM], BF16)) for i in range(2)]
            dnb = [ph.enter_context(nc.sbuf_tensor("p4_dnb%d" % i, [128, 8 * GD], BF16)) for i in range(2)]
            hid = ph.enter_context(nc.sbuf_tensor("p4_hid", [128, 8 * S], BF16))
            gsc = ph.enter_context(nc.sbuf_tensor("p4_gsc", [128, S], F32))
            ybuf = [ph.enter_context(nc.sbuf_tensor("p4_y%d" % i, [128, S], F32)) for i in range(2)]
            ps_h = [ph.enter_context(nc.psum_tensor("p4_ps%d" % i, [128, S], F32)) for i in range(2)]
            gld = sem("p4_gld")
            wld = sem("p4_wld")
            cast = sem("p4_cast")
            nrm = sem("p4_nrm")
            mmu = sem("p4_mmu")
            mmd2 = sem("p4_mmd")
            ysc = sem("p4_ysc")
            pe0 = sem("p4_pe0")
            hev = sem("p4_hev")

            def bc2(e):
                last = None
                for n in range(4):
                    last = e.matmul(ps_h[0][:, n * 512:(n + 1) * 512], lhsT=ones_row[:],
                                    rhs=a2_row[0:1, n * 512:(n + 1) * 512], start=True, stop=True)
                for n in range(4):
                    last = e.matmul(ps_h[1][:, n * 512:(n + 1) * 512], lhsT=ones_row[:],
                                    rhs=b2_row[0:1, n * 512:(n + 1) * 512], start=True, stop=True)
                return last
            pl.step("tensor", bc2, incs=[(pe0, 1)])
            pl.step("vector", lambda v: v.tensor_copy(a2b[:], ps_h[0][:]),
                    waits=[(pe0, 1)], incs=[(hev, 1)])
            pl.step("vector", lambda v: v.tensor_copy(b2b[:], ps_h[1][:]),
                    incs=[(hev, 1)])

            hps_user = 2
            for g in range(NG):
                wpar = g % 2
                waits = [(mmd2, 2 * (g - 1))] if g >= 2 else []
                pl.step("gpsimd", lambda e, _g=g, _p=wpar: e.dma_start(
                    out=upb[_p][:], in_=up_dev[_g]), waits=waits, incs=[(wld, 16)])
                pl.step("gpsimd", lambda e, _g=g, _p=wpar: e.dma_start(
                    out=dnb[_p][:], in_=down_dev[_g]), incs=[(wld, 16)])
                for c in range(2):
                    gi = (2 * g + c) % 4
                    waits = []
                    if 2 * g + c >= 4:
                        waits.append((nrm, 2 * g + c - 3))
                    pl.step("gpsimd", lambda e, _g=g, _c=c, _gi=gi: e.indirect_dma_start(
                        out=gbuf[_gi][:], out_offset=None, in_=OUT,
                        in_offset=bass.IndirectOffsetOnAxis(
                            perm_sb[:, 2 * _g + _c:2 * _g + _c + 1], 0)),
                        waits=waits, incs=[(gld, 16)])
                    waits = [(gld, gld.n), (hev, 2)]
                    if 2 * g + c >= 4:
                        waits.append((mmu, 8 * (g - 1)))

                    def norm(v, _gi=gi):
                        v.tensor_tensor(out=gsc[:], in0=gbuf[_gi][:], in1=a2b[:], op=ALU.mult)
                        v.drain()
                        return v.tensor_tensor(out=tgb[_gi][:], in0=gsc[:], in1=b2b[:],
                                               op=ALU.add)
                    pl.step("vector", norm, waits=waits, incs=[(nrm, 1)])
                for ms in range(8):
                    ppar = hps_user % 2
                    waits = [(wld, wld.n), (nrm, nrm.n), (hev, hps_user - 1)]

                    def mmup(e, _g=g, _ms=ms, _p=wpar, _pp=ppar):
                        last = None
                        for n in range(4):
                            for c in range(2):
                                last = e.matmul(
                                    ps_h[_pp][:, n * 512:(n + 1) * 512],
                                    lhsT=upb[_p][:, c * GM + _ms * 128: c * GM + _ms * 128 + 128],
                                    rhs=tgb[(2 * _g + c) % 4][:, n * 512:(n + 1) * 512],
                                    start=(c == 0), stop=(c == 1))
                        return last
                    pl.step("tensor", mmup, waits=waits, incs=[(mmu, 1)])
                    pl.step("scalar", lambda e, _ms=ms, _pp=ppar: e.activation(
                        out=hid[:, _ms * S:(_ms + 1) * S], in_=ps_h[_pp][:], func=AF.Gelu),
                        waits=[(mmu, mmu.n), (mmd2, mmd2.n)], incs=[(hev, 1)], war=True)
                    hps_user += 1
                for os in range(2):
                    ppar = hps_user % 2
                    ypar = (2 * g + os) % 2
                    waits = [(hev, hev.n), (hev, hps_user - 1)]

                    def mmdn(e, _os=os, _p=wpar, _pp=ppar):
                        last = None
                        for n in range(4):
                            for c in range(8):
                                last = e.matmul(
                                    ps_h[_pp][:, n * 512:(n + 1) * 512],
                                    lhsT=dnb[_p][:, c * GD + _os * 128: c * GD + _os * 128 + 128],
                                    rhs=hid[:, c * S + n * 512: c * S + (n + 1) * 512],
                                    start=(c == 0), stop=(c == 7))
                        return last
                    pl.step("tensor", mmdn, waits=waits, incs=[(mmd2, 1)])
                    waits = [(mmd2, mmd2.n)]
                    if 2 * g + os >= 2:
                        waits.append((ysc, 16 * (2 * g + os - 1)))
                    pl.step("vector", lambda v, _yp=ypar, _pp=ppar: v.tensor_copy(
                        ybuf[_yp][:], ps_h[_pp][:]),
                        waits=waits, incs=[(hev, 1)])
                    hps_user += 1
                    pl.step("gpsimd", lambda e, _g=g, _os=os, _yp=ypar: e.indirect_dma_start(
                        out=OUT, out_offset=bass.IndirectOffsetOnAxis(
                            perm_sb[:, 2 * _g + _os:2 * _g + _os + 1], 0),
                        in_=ybuf[_yp][:], in_offset=None, oob_is_err=True,
                        compute_op=ALU.add),
                        waits=[(hev, hev.n)], incs=[(ysc, 16)])
            pl.step("gpsimd", lambda e: e.nop(), waits=[(ysc, ysc.n)])
            block = ph.enter_context(nc.Block())
            pl.install(block)
        ln2scope.close()

    return nc


def prep_inputs(L0_attn_hidden_states, L0_attn_wq, L0_attn_bq, L0_attn_wk, L0_attn_bk,
                L0_attn_wv, L0_attn_bv, L0_attn_wo, L0_attn_bo, L0_ffn_up, L0_ffn_down,
                L0_ffn_perm):
    def slabs(w):
        wt = np.ascontiguousarray(np.asarray(w, np.float32).T)
        return np.ascontiguousarray(wt.reshape(H, NMT, 128).transpose(1, 0, 2))

    def bias_lay(bv):
        return np.ascontiguousarray(np.asarray(bv, np.float32).reshape(NMT, 128).T)

    shared = {
        "wq_dev": slabs(L0_attn_wq), "wk_dev": slabs(L0_attn_wk),
        "wv_dev": slabs(L0_attn_wv), "wo_dev": slabs(L0_attn_wo),
        "bq_dev": bias_lay(L0_attn_bq), "bk_dev": bias_lay(L0_attn_bk),
        "bv_dev": bias_lay(L0_attn_bv), "bo_dev": bias_lay(L0_attn_bo),
        "up_dev": np.ascontiguousarray(
            np.asarray(L0_ffn_up, np.float32).reshape(NG, 2, 128, GM)
            .transpose(0, 2, 1, 3).reshape(NG, 128, 2 * GM)),
        "down_dev": np.ascontiguousarray(
            np.asarray(L0_ffn_down, np.float32).reshape(NG, 8, 128, GD)
            .transpose(0, 2, 1, 3).reshape(NG, 128, 8 * GD)),
        "perm_dev": np.ascontiguousarray(
            np.asarray(L0_ffn_perm).astype(np.uint32).reshape(32, 128).T),
        "ident_b": np.eye(128, dtype=ml_dtypes.bfloat16),
        "ident_f": np.eye(128, dtype=np.float32),
    }
    x = np.asarray(L0_attn_hidden_states, np.float32)
    in_maps = []
    for ci in range(8):
        m = dict(shared)
        m["x0T"] = np.ascontiguousarray(x[ci].T)
        in_maps.append(m)
    return in_maps


def kernel(**inputs):
    in_maps = prep_inputs(**inputs)
    nc = bass.Bass("TRN2", target_bir_lowering=False)
    build_program(nc)
    res = run_bass_kernel_spmd(nc, in_maps, core_ids=list(range(8)))
    out = np.stack([np.asarray(r["OUT"]).T for r in res.results], axis=0)
    return np.ascontiguousarray(out.astype(np.float32))




# version 19 (final)
# speedup vs baseline: 1.2994x; 1.2994x over previous
import math
import numpy as np
import ml_dtypes
from contextlib import ExitStack

import concourse.bass as bass
import concourse.mybir as mybir
from concourse.bass_utils import run_bass_kernel_spmd

F32 = mybir.dt.float32
BF16 = mybir.dt.bfloat16
U32 = mybir.dt.uint32
AF = mybir.ActivationFunctionType
ALU = mybir.AluOpType
AX = mybir.AxisListType

H = 4096
S = 2048
NH = 32
HD = 128
NG = 16
GD = 256
GM = 1024
TOPK = 409
K1 = 6
BRACKET = 16.0
BHI = 6.0
BSPAN = BRACKET - BHI
QSCALE = 1.0 / math.sqrt(HD)
NMT = H // 128
ACT_TILES = 7
EPS = 1e-5


class Sem:
    __slots__ = ("h", "n")

    def __init__(self, h):
        self.h = h
        self.n = 0


class Plan:
    def __init__(self):
        self.ops = []

    def step(self, engine, build, waits=(), incs=(), war=False):
        wl = [(s.h, int(v)) for (s, v) in waits]
        il = [(s.h, int(k)) for (s, k) in incs]
        for (s, k) in incs:
            s.n += k

        def f(eng):
            for (hh, v) in wl:
                eng.wait_ge(hh, v)
            inst = build(eng)
            if il:
                assert inst is not None, "inc on step with no instruction"
                assert len(il) == 1, "one semaphore update per instruction"
                if engine in ("vector", "scalar") and not war:
                    eng.drain().then_inc(*il[0])
                else:
                    inst.then_inc(*il[0])
        self.ops.append((engine, f))

    def install(self, block):
        for name in ("sync", "vector", "scalar", "tensor", "gpsimd"):
            mine = [f for (e, f) in self.ops if e == name]
            if not mine:
                continue

            def body(eng, _mine=mine):
                for f in _mine:
                    f(eng)
            getattr(block, name)(body)


def build_program(nc, debug=False, upto=5):
    x0T = nc.dram_tensor("x0T", [H, S], F32, kind="ExternalInput").ap()
    wdev = {}
    for nm in ("wq", "wk", "wv", "wo"):
        wdev[nm] = nc.dram_tensor(nm + "_dev", [NMT, H, 128], F32, kind="ExternalInput").ap()
    bdev = {}
    for nm in ("bq", "bk", "bv", "bo"):
        bdev[nm] = nc.dram_tensor(nm + "_dev", [128, NMT], F32, kind="ExternalInput").ap()
    up_dev = nc.dram_tensor("up_dev", [NG, 128, 2 * GM], F32, kind="ExternalInput").ap()
    down_dev = nc.dram_tensor("down_dev", [NG, 128, 8 * GD], F32, kind="ExternalInput").ap()
    perm_dev = nc.dram_tensor("perm_dev", [128, 32], U32, kind="ExternalInput").ap()
    ident_b = nc.dram_tensor("ident_b", [128, 128], BF16, kind="ExternalInput").ap()
    ident_f = nc.dram_tensor("ident_f", [128, 128], F32, kind="ExternalInput").ap()
    OUT = nc.dram_tensor("OUT", [H, S], F32, kind="ExternalOutput").ap()

    skind = "ExternalOutput" if debug else "Internal"
    qT_d = nc.dram_tensor("qT_d", [H, S], BF16, kind=skind).ap()
    kT_d = nc.dram_tensor("kT_d", [H, S], BF16, kind=skind).ap()
    vT_d = nc.dram_tensor("vT_d", [H, S], BF16, kind=skind).ap()
    ctxT_d = nc.dram_tensor("ctxT_d", [H, S], BF16, kind=skind).ap()
    x1T_d = nc.dram_tensor("x1T_d", [H, S], F32, kind=skind).ap()
    yT_d = nc.dram_tensor("yT_d", [H, S], F32, kind=skind).ap()

    with ExitStack() as prog:
        sb = lambda name, shape, dt: prog.enter_context(nc.sbuf_tensor(name, shape, dt))
        sem = lambda name: Sem(prog.enter_context(nc.semaphore(name)))

        id_b = sb("id_b", [128, 128], BF16)
        id_f = sb("id_f", [128, 128], F32)
        ones_col = sb("ones_col", [128, 1], F32)
        ones_row = sb("ones_row", [1, 128], F32)
        c8 = sb("c8", [128, 8], F32)
        eps1 = sb("eps1", [1, 1], F32)
        f1 = sb("f1", [1, 1], F32)
        bias_sb = {nm: sb(nm + "_sb", [128, NMT], F32) for nm in ("bq", "bk", "bv", "bo")}
        perm_sb = sb("perm_sb", [128, 32], U32)

        s_init = sem("s_init")

        ln1scope = ExitStack()
        a1_row = ln1scope.enter_context(nc.sbuf_tensor("a1_row", [1, S], F32))
        b1_row = ln1scope.enter_context(nc.sbuf_tensor("b1_row", [1, S], F32))
        xln1 = ln1scope.enter_context(nc.sbuf_tensor("xln1", [128, NMT * S], BF16))
        def emit_ln_stats(tag, src_dram, a_row, b_row, first_steps=None):
            with ExitStack() as ph:
                pl = Plan()
                xbuf = [ph.enter_context(nc.sbuf_tensor(tag + "_x%d" % i, [128, S], F32))
                        for i in range(2)]
                sqbuf = [ph.enter_context(nc.sbuf_tensor(tag + "_sq%d" % i, [128, S], F32))
                         for i in range(2)]
                srow = ph.enter_context(nc.sbuf_tensor(tag + "_srow", [1, 2 * S], F32))
                ps_s1 = ph.enter_context(nc.psum_tensor(tag + "_s1", [1, S], F32))
                ps_s2 = ph.enter_context(nc.psum_tensor(tag + "_s2", [1, S], F32))
                ld = sem(tag + "_ld")
                sq = sem(tag + "_sq")
                pe = sem(tag + "_pe")
                dve = sem(tag + "_dve")
                act2 = sem(tag + "_act2")
                if first_steps is not None:
                    first_steps(pl, ld)
                for c in range(NMT):
                    bi = c % 2
                    waits = [(pe, (c - 1))] if c >= 2 else []
                    pl.step("sync", lambda e, _c=c, _b=bi: e.dma_start(
                        out=xbuf[_b][:], in_=src_dram[_c * 128:(_c + 1) * 128, :]),
                        waits=waits, incs=[(ld, 16)])
                    pl.step("scalar", lambda e, _b=bi: e.activation(
                        out=sqbuf[_b][:], in_=xbuf[_b][:], func=AF.Square),
                        waits=[(ld, ld.n)], incs=[(sq, 1)], war=True)

                    def mm_s(e, _c=c, _b=bi):
                        last = None
                        for n in range(4):
                            last = e.matmul(ps_s1[0:1, n * 512:(n + 1) * 512],
                                            lhsT=ones_col[:],
                                            rhs=xbuf[_b][:, n * 512:(n + 1) * 512],
                                            start=(_c == 0), stop=(_c == NMT - 1))
                        for n in range(4):
                            last = e.matmul(ps_s2[0:1, n * 512:(n + 1) * 512],
                                            lhsT=ones_col[:],
                                            rhs=sqbuf[_b][:, n * 512:(n + 1) * 512],
                                            start=(_c == 0), stop=(_c == NMT - 1))
                        return last
                    pl.step("tensor", mm_s,
                            waits=[(ld, ld.n), (sq, sq.n), (s_init, 1)],
                            incs=[(pe, 1)])

                mu = srow[0:1, 0:S]
                ms = srow[0:1, S:2 * S]
                var = sqbuf[0][0:1, 0:S]

                def stats_evac(v):
                    v.tensor_scalar(out=mu, in0=ps_s1[:], scalar1=1.0 / H,
                                    scalar2=None, op0=ALU.mult)
                    v.tensor_scalar(out=ms, in0=ps_s2[:], scalar1=1.0 / H,
                                    scalar2=None, op0=ALU.mult)
                    v.drain()
                    v.tensor_tensor(out=b_row[:], in0=mu, in1=mu, op=ALU.mult)
                    v.drain()
                    return v.tensor_tensor(out=var, in0=ms, in1=b_row[:], op=ALU.subtract)
                pl.step("vector", stats_evac, waits=[(pe, pe.n)], incs=[(dve, 1)])

                def rstd_act(e):
                    e.activation(out=b_row[:], in_=var, func=AF.Ln, bias=eps1[:])
                    e.drain()
                    return e.activation(out=a_row[:], in_=b_row[:], func=AF.Exp, scale=-0.5)
                pl.step("scalar", rstd_act, waits=[(dve, dve.n)], incs=[(act2, 1)])

                def b_calc(v):
                    v.tensor_tensor(out=b_row[:], in0=mu, in1=a_row[:], op=ALU.mult)
                    v.drain()
                    return v.tensor_scalar(out=b_row[:], in0=b_row[:], scalar1=-1.0,
                                           scalar2=None, op0=ALU.mult)
                pl.step("vector", b_calc, waits=[(act2, act2.n)], incs=[(dve, 1)])

                block = ph.enter_context(nc.Block())
                pl.install(block)

        def p0_first_steps(pl, ld):
            def init_consts(v):
                v.memset(ones_col[:], 1.0)
                v.memset(ones_row[:], 1.0)
                v.memset(eps1[:], EPS)
                v.memset(f1[:], 0.0)
                last = None
                for j in range(8):
                    last = v.memset(c8[:, j:j + 1], float(j))
                return last
            pl.step("vector", init_consts, incs=[(s_init, 1)])
            pl.step("sync", lambda e: e.dma_start(out=id_b[:], in_=ident_b), incs=[(ld, 16)])
            pl.step("sync", lambda e: e.dma_start(out=id_f[:], in_=ident_f), incs=[(ld, 16)])
            pl.step("sync", lambda e: e.dma_start(out=perm_sb[:], in_=perm_dev), incs=[(ld, 16)])
            for nm in ("bq", "bk", "bv", "bo"):
                pl.step("sync", lambda e, _n=nm: e.dma_start(out=bias_sb[_n][:], in_=bdev[_n]),
                        incs=[(ld, 16)])

        emit_ln_stats("p0", x0T, a1_row, b1_row, first_steps=p0_first_steps)

        if True:
            with ExitStack() as ph:
                pl = Plan()
                xbuf = [ph.enter_context(nc.sbuf_tensor("p0b_x%d" % i, [128, S], F32)) for i in range(2)]
                tmpb = [ph.enter_context(nc.sbuf_tensor("p0b_t%d" % i, [128, S], F32)) for i in range(2)]
                a1b = ph.enter_context(nc.sbuf_tensor("a1b", [128, S], F32))
                b1b = ph.enter_context(nc.sbuf_tensor("b1b", [128, S], F32))
                ps_a = ph.enter_context(nc.psum_tensor("p0b_a", [128, S], F32))
                ps_b = ph.enter_context(nc.psum_tensor("p0b_b", [128, S], F32))
                ld = sem("p0b_ld")
                pe = sem("p0b_pe")
                dve = sem("p0b_dve")
                nrm = sem("p0b_nrm")

                def bc_mm(e):
                    last = None
                    for n in range(4):
                        last = e.matmul(ps_a[:, n * 512:(n + 1) * 512], lhsT=ones_row[:],
                                        rhs=a1_row[0:1, n * 512:(n + 1) * 512], start=True, stop=True)
                    for n in range(4):
                        last = e.matmul(ps_b[:, n * 512:(n + 1) * 512], lhsT=ones_row[:],
                                        rhs=b1_row[0:1, n * 512:(n + 1) * 512], start=True, stop=True)
                    return last
                pl.step("tensor", bc_mm, incs=[(pe, 1)])

                def bc_evac(v):
                    v.tensor_copy(a1b[:], ps_a[:])
                    return v.tensor_copy(b1b[:], ps_b[:])
                pl.step("vector", bc_evac, waits=[(pe, 1)], incs=[(dve, 1)])

                for c in range(NMT):
                    bi = c % 2
                    waits = [(nrm, (c - 1))] if c >= 2 else []
                    pl.step("sync", lambda e, _c=c, _b=bi: e.dma_start(
                        out=xbuf[_b][:], in_=x0T[_c * 128:(_c + 1) * 128, :]),
                        waits=waits, incs=[(ld, 16)])

                    def norm(v, _c=c, _b=bi):
                        v.tensor_tensor(out=tmpb[_b][:], in0=xbuf[_b][:], in1=a1b[:], op=ALU.mult)
                        v.drain()
                        return v.tensor_tensor(out=xln1[:, _c * S:(_c + 1) * S],
                                               in0=tmpb[_b][:], in1=b1b[:], op=ALU.add)
                    pl.step("vector", norm, waits=[(ld, ld.n), (dve, 1)],
                            incs=[(nrm, 1)])

                block = ph.enter_context(nc.Block())
                pl.install(block)

            if upto >= 1:
              with ExitStack() as ph:
                pl = Plan()
                wb = [ph.enter_context(nc.sbuf_tensor("p1_wb%d" % i, [128, H], BF16)) for i in range(2)]
                otile = [ph.enter_context(nc.sbuf_tensor("p1_o%d" % i, [128, S], BF16)) for i in range(2)]
                ps_o = [ph.enter_context(nc.psum_tensor("p1_ps%d" % i, [128, S], F32)) for i in range(2)]
                ld = sem("p1_ld")
                cast = sem("p1_cast")
                mmd = sem("p1_mm")
                ev = sem("p1_ev")
                st = sem("p1_st")

                mats = [("wq", "bq", qT_d), ("wk", "bk", kT_d), ("wv", "bv", vT_d)]
                k = 0
                for (wn, bn, dst) in mats:
                    for mt in range(NMT):
                        bi = k % 2
                        waits = [(mmd, k - 1)] if k >= 2 else []
                        pl.step("gpsimd", lambda e, _w=wn, _mt=mt, _b=bi: e.dma_start(
                            out=wb[_b][:],
                            in_=wdev[_w][_mt].rearrange("(c p) o -> c p o", p=128).transpose([1, 0, 2])),
                            waits=waits, incs=[(ld, 16)])
                        waits = [(ld, ld.n)]
                        if k >= 2:
                            waits.append((ev, k - 1))

                        def mm(e, _b=bi):
                            last = None
                            for n in range(4):
                                for c in range(NMT):
                                    last = e.matmul(
                                        ps_o[_b][:, n * 512:(n + 1) * 512],
                                        lhsT=wb[_b][:, c * 128:(c + 1) * 128],
                                        rhs=xln1[:, c * S + n * 512: c * S + (n + 1) * 512],
                                        start=(c == 0), stop=(c == NMT - 1))
                            return last
                        pl.step("tensor", mm, waits=waits, incs=[(mmd, 1)])
                        waits = [(mmd, mmd.n)]
                        if k >= 2:
                            waits.append((st, 16 * (k - 1)))
                        pl.step("vector", lambda v, _b=bi, _bn=bn, _mt=mt: v.tensor_scalar(
                            out=otile[_b][:], in0=ps_o[_b][:],
                            scalar1=bias_sb[_bn][:, _mt:_mt + 1], scalar2=None, op0=ALU.add),
                            waits=waits, incs=[(ev, 1)])
                        pl.step("sync", lambda e, _b=bi, _d=dst, _mt=mt: e.dma_start(
                            out=_d[_mt * 128:(_mt + 1) * 128, :], in_=otile[_b][:]),
                            waits=[(ev, ev.n)], incs=[(st, 16)])
                        k += 1
                pl.step("sync", lambda e: e.nop(), waits=[(st, st.n)])
                block = ph.enter_context(nc.Block())
                pl.install(block)
        if debug and upto == 0:
            with ExitStack() as ph:
                pl = Plan()
                dst = sem("dbg0_st")
                for c in range(NMT):
                    pl.step("sync", lambda e, _c=c: e.dma_start(
                        out=qT_d[_c * 128:(_c + 1) * 128, :],
                        in_=xln1[:, _c * S:(_c + 1) * S]), incs=[(dst, 16)])
                pl.step("sync", lambda e: e.nop(), waits=[(dst, dst.n)])
                block = ph.enter_context(nc.Block())
                pl.install(block)
        ln1scope.close()
        if upto < 2:
            return nc

        with ExitStack() as ph:
            pl = Plan()
            ET = [ph.enter_context(nc.sbuf_tensor("p2_E%d" % i, [128, 8 * S], F32)) for i in range(2)]
            qh = ph.enter_context(nc.sbuf_tensor("p2_q", [128, S], BF16))
            kh = ph.enter_context(nc.sbuf_tensor("p2_k", [128, S], BF16))
            vh = ph.enter_context(nc.sbuf_tensor("p2_v", [128, S], BF16))
            Vt = [ph.enter_context(nc.sbuf_tensor("p2_V%d" % i, [128, S], BF16)) for i in range(2)]
            sc1 = ph.enter_context(nc.sbuf_tensor("p2_sc1", [128, S], F32))
            sc2 = ph.enter_context(nc.sbuf_tensor("p2_sc2", [128, S], F32))
            sc_act = ph.enter_context(nc.sbuf_tensor("p2_scact", [128, S], BF16))
            Pbf = [ph.enter_context(nc.sbuf_tensor("p2_P%d" % i, [128, S], BF16)) for i in range(2)]
            PT4 = ph.enter_context(nc.sbuf_tensor("p2_PT4", [128, 16 * 512], BF16))
            ctxh = ph.enter_context(nc.sbuf_tensor("p2_ctxh", [128, S], BF16))
            zb_sb = ph.enter_context(nc.sbuf_tensor("p2_zb", [128, 512], F32))
            zrow = ph.enter_context(nc.sbuf_tensor("p2_zrow", [1, 512], F32))
            top8 = ph.enter_context(nc.sbuf_tensor("p2_top8", [128, 8], F32))
            t8t = ph.enter_context(nc.sbuf_tensor("p2_t8t", [128, 8], F32))

            def stt(i, name):
                return ph.enter_context(nc.sbuf_tensor("p2_%s%d" % (name, i), [128, 8], F32))
            st_zsum = [stt(i, "zsum") for i in range(2)]
            st_tlo = [[stt(i, "tlo%d" % j) for i in range(2)] for j in range(2)]
            st_thi = [[stt(i, "thi%d" % j) for i in range(2)] for j in range(2)]
            st_chi = [[stt(i, "chi%d" % j) for i in range(2)] for j in range(2)]
            st_cnt = [stt(i, "cnt") for i in range(2)]
            st_zsum2 = [ph.enter_context(nc.sbuf_tensor("p2_zs2_%d" % i, [128, 16], F32))
                        for i in range(2)]
            st_tmid = [stt(i, "tmid") for i in range(2)]
            st_tm2 = [stt(i, "tm2") for i in range(2)]
            st_ge = [ph.enter_context(nc.sbuf_tensor("p2_ge%d" % i, [128, 8], U32)) for i in range(2)]
            st_gn = [ph.enter_context(nc.sbuf_tensor("p2_gn%d" % i, [128, 8], U32)) for i in range(2)]
            st_u2s = [stt(i, "u2s") for i in range(2)]
            st_u = [stt(i, "u") for i in range(2)]
            st_ulo = [stt(i, "ulo") for i in range(2)]
            st_uhi = [stt(i, "uhi") for i in range(2)]
            st_rm1 = [stt(i, "rm1") for i in range(2)]
            st_tst = [stt(i, "tst") for i in range(2)]
            st_tst2 = [stt(i, "tst2") for i in range(2)]
            st_zsel = [stt(i, "zsel") for i in range(2)]
            st_zinv = [stt(i, "zinv") for i in range(2)]

            ps_s = ph.enter_context(nc.psum_tensor("p2_psS", [128, S], F32))
            ps_pt = ph.enter_context(nc.psum_tensor("p2_pspt", [128, 1024], BF16))
            ps_cx = [ph.enter_context(nc.psum_tensor("p2_pscx%d" % i, [128, 512], F32)) for i in range(2)]
            ps_aux = ph.enter_context(nc.psum_tensor("p2_psaux", [128, 512], F32))

            s_ld = sem("p2_ld")
            s_vtr = sem("p2_vtr")
            s_vev = sem("p2_vev")
            s_smm = sem("p2_smm")
            s_eev = sem("p2_eev")
            s_tmid = sem("p2_tmid")
            s_u = sem("p2_u")
            s_acts = sem("p2_acts")
            s_fin = sem("p2_fin")
            s_ptt = sem("p2_ptt")
            s_ptev = sem("p2_ptev")
            s_pv = sem("p2_pv")
            s_trc = sem("p2_trc")
            s_zinv = sem("p2_zinv")
            s_zrow = sem("p2_zrow")
            s_zbm = sem("p2_zbm")
            s_zbev = sem("p2_zbev")
            s_ctx = sem("p2_ctx")
            s_cst = sem("p2_cst")

            NU = NH * 2
            ld_head = {}

            def dma_head(h):
                waits = []
                if h >= 1:
                    waits = [(s_smm, 32 * h), (s_vtr, 2 * h)]
                for (dst, src) in ((qh, qT_d), (kh, kT_d), (vh, vT_d)):
                    pl.step("sync", lambda e, _d=dst, _s=src, _h=h: e.dma_start(
                        out=_d[:], in_=_s[_h * 128:(_h + 1) * 128, :]),
                        waits=waits, incs=[(s_ld, 16)])
                    waits = []
                ld_head[h] = s_ld.n

            def emit_vtr(h):
                par = h % 2
                for half in range(2):
                    def tr(e, _hf=half):
                        last = None
                        for t in range(8):
                            kc = _hf * 8 + t
                            last = e.transpose(ps_pt[:, t * 128:(t + 1) * 128],
                                               vh[:, kc * 128:(kc + 1) * 128], id_b[:])
                        return last
                    pl.step("tensor", tr,
                            waits=[(s_ld, ld_head[h]), (s_ptev, s_ptev.n)],
                            incs=[(s_vtr, 1)])
                    pl.step("vector", lambda v, _p=par, _hf=half: v.tensor_copy(
                        Vt[_p][:, _hf * 1024:(_hf + 1) * 1024], ps_pt[:]),
                        waits=[(s_vtr, s_vtr.n)], incs=[(s_ptev, 1)], war=True)

            def emit_stage_S(u):
                h, hf = u // 2, u % 2
                par = u % 2
                for j in range(8):
                    qt = hf * 8 + j
                    for half in range(2):
                        def smm(e, _h=h, _qt=qt, _hf=half):
                            last = None
                            for n in range(2 * _hf, 2 * _hf + 2):
                                last = e.matmul(ps_s[:, n * 512:(n + 1) * 512],
                                                lhsT=qh[:, _qt * 128:(_qt + 1) * 128],
                                                rhs=kh[:, n * 512:(n + 1) * 512],
                                                start=True, stop=True)
                            return last
                        pl.step("tensor", smm,
                                waits=[(s_ld, ld_head[h]),
                                       (s_eev, max(0, s_eev.n - 1))],
                                incs=[(s_smm, 1)])
                        waits = [(s_smm, s_smm.n)]
                        if u >= 2 and half == 0:
                            waits.append((s_fin, 8 * (u - 2) + j + 1))
                        pl.step("scalar", lambda e, _par=par, _j=j, _hf=half: e.activation(
                            out=ET[_par][:, _j * S + _hf * 1024:_j * S + _hf * 1024 + 1024],
                            in_=ps_s[:, _hf * 1024:(_hf + 1) * 1024], func=AF.Exp,
                            scale=QSCALE,
                            accum_out=st_zsum2[_par][:, 2 * _j + _hf:2 * _j + _hf + 1]),
                            waits=waits, incs=[(s_eev, 1)],
                            war=not (j == 7 and half == 1))

            def emit_bisect(u, fill=()):
                fill = list(fill)
                total = sum(c for c, _ in fill)
                budget = total / K1
                spent = 0.0
                par = u % 2
                uu, hi, ch = st_tlo[0][par], st_thi[0][par], st_chi[0][par]
                uu2, hi2, ch2 = st_tlo[1][par], st_thi[1][par], st_chi[1][par]
                while fill and spent < total * 0.25:
                    c, fn = fill.pop(0)
                    fn()
                    spent += c
                spent = 0.0
                budget = sum(c for c, _ in fill) / K1

                def init_d(v, _p=par, _u=uu, _hi=hi, _ch=ch):
                    v.drain()
                    z2 = st_zsum2[_p][:].rearrange("p (c t) -> p c t", t=2)
                    z1 = st_zsum[_p][:].rearrange("p (c t) -> p c t", t=1)
                    v.tensor_tensor(out=z1, in0=z2[:, :, 0:1], in1=z2[:, :, 1:2],
                                    op=ALU.add)
                    v.drain()
                    v.tensor_scalar(out=_u[:], in0=st_zsum[_p][:],
                                    scalar1=math.exp(-(BRACKET + BHI) / 2),
                                    scalar2=None, op0=ALU.mult)
                    return v.memset(_ch[:], 0.0)
                pl.step("vector", init_d, waits=[(s_eev, s_eev.n)], incs=[(s_u, 1)])

                for it in range(K1):
                    w = (BSPAN / 4) / (2 ** it)
                    EW = math.exp(-w)
                    DW = math.exp(w) - EW
                    for j in range(ACT_TILES):
                        waits = [(s_u, s_u.n)] if j == 0 else []
                        incs = [(s_acts, 1)] if j == ACT_TILES - 1 else []
                        pl.step("scalar", lambda e, _p=par, _j=j, _u=uu: e.activation(
                            out=sc_act[:], in_=ET[_p][:, _j * S:(_j + 1) * S], func=AF.Sign,
                            scale=-1.0, bias=_u[:, _j:_j + 1],
                            accum_out=st_cnt[_p][:, _j:_j + 1]),
                            waits=waits, incs=incs, war=True)
                    for j in range(ACT_TILES, 8):
                        pl.step("vector", lambda v, _p=par, _j=j, _u=uu: v.tensor_scalar(
                            out=sc2[:], in0=ET[_p][:, _j * S:(_j + 1) * S],
                            scalar1=_u[:, _j:_j + 1], scalar2=None,
                            op0=ALU.is_ge, op1=ALU.add,
                            accum_out=st_cnt[_p][:, _j:_j + 1]))

                    while fill and spent < budget * (it + 1):
                        c, fn = fill.pop(0)
                        fn()
                        spent += c

                    def upd_crit(v, _p=par, _u=uu, _u2=uu2, _EW=EW, _DW=DW):
                        v.tensor_scalar(out=st_u[_p][:, 0:ACT_TILES],
                                        in0=st_cnt[_p][:, 0:ACT_TILES],
                                        scalar1=float(2048 - 2 * TOPK), scalar2=_DW,
                                        op0=ALU.is_le, op1=ALU.mult)
                        v.tensor_scalar(out=st_u[_p][:, ACT_TILES:8],
                                        in0=st_cnt[_p][:, ACT_TILES:8],
                                        scalar1=float(TOPK), scalar2=_DW,
                                        op0=ALU.is_ge, op1=ALU.mult)
                        v.drain()
                        return v.scalar_tensor_tensor(
                            out=_u2[:], in0=st_u[_p][:], scalar=_EW,
                            in1=_u[:], op0=ALU.add, op1=ALU.mult)
                    pl.step("vector", upd_crit, waits=[(s_acts, s_acts.n)],
                            incs=[(s_u, 1)])

                    last = (it == K1 - 1)
                    def upd_rest(v, _p=par, _u=uu, _u2=uu2, _hi=hi, _ch=ch,
                                 _hi2=hi2, _ch2=ch2, _last=last):
                        v.tensor_scalar(out=st_tm2[_p][:, 0:ACT_TILES],
                                        in0=st_cnt[_p][:, 0:ACT_TILES],
                                        scalar1=-0.5, scalar2=1024.0,
                                        op0=ALU.mult, op1=ALU.add)
                        v.tensor_scalar(out=st_tm2[_p][:, ACT_TILES:8],
                                        in0=st_cnt[_p][:, ACT_TILES:8],
                                        scalar1=1.0, scalar2=None, op0=ALU.mult)
                        v.tensor_scalar(out=st_ge[_p][:, 0:ACT_TILES],
                                        in0=st_cnt[_p][:, 0:ACT_TILES],
                                        scalar1=float(2048 - 2 * TOPK), scalar2=None,
                                        op0=ALU.is_le)
                        v.tensor_scalar(out=st_ge[_p][:, ACT_TILES:8],
                                        in0=st_cnt[_p][:, ACT_TILES:8],
                                        scalar1=float(TOPK), scalar2=None, op0=ALU.is_ge)
                        r = v.select(_ch2[:], st_ge[_p][:], _ch[:],
                                     st_tm2[_p][:], add_drain=True)
                        if _last:
                            r = v.tensor_scalar(
                                out=st_ulo[_p][:], in0=_u2[:],
                                scalar1=math.exp(-(BSPAN / 2) / (2 ** K1)),
                                scalar2=None, op0=ALU.mult)
                            r = v.tensor_scalar(
                                out=st_uhi[_p][:], in0=_u2[:],
                                scalar1=math.exp((BSPAN / 2) / (2 ** K1)),
                                scalar2=None, op0=ALU.mult)
                        return r
                    pl.step("vector", upd_rest,
                            incs=([(s_tmid, 1)] if last else []))
                    uu, uu2 = uu2, uu
                    ch, ch2 = ch2, ch
                for _c, fn in fill:
                    fn()
                return uu, st_uhi[par], ch

            def make_tail_thunks(u, ch, hi):
                thunks = []
                par = u % 2
                def rm1_f(v, _p=par, _ch=ch):
                    v.drain()
                    v.tensor_scalar(out=st_rm1[_p][:], in0=_ch[:], scalar1=-1.0,
                                    scalar2=408.0, op0=ALU.mult, op1=ALU.add)
                    return v.drain()
                thunks.append((2.3, lambda _f=rm1_f, _w=s_tmid.n: pl.step(
                    "vector", _f, waits=[(s_tmid, _w)])))
                for j in range(8):
                    def r2(v, _p=par, _j=j, _hi=hi):
                        v.scalar_tensor_tensor(
                            out=sc2[:], in0=ET[_p][:, _j * S:(_j + 1) * S],
                            scalar=_hi[:, _j:_j + 1],
                            in1=ET[_p][:, _j * S:(_j + 1) * S],
                            op0=ALU.is_lt, op1=ALU.mult)
                        v.drain()
                        v.max(out=top8[:], in_=sc2[:])
                        v.drain()
                        return v.scalar_tensor_tensor(
                            out=t8t[:], in0=c8[:], scalar=st_rm1[_p][:, _j:_j + 1],
                            in1=top8[:], op0=ALU.is_equal, op1=ALU.mult,
                            accum_out=st_tst[_p][:, _j:_j + 1])
                    thunks.append((2.3, lambda _f=r2: pl.step("vector", _f)))
                def tstmax_f(v, _p=par):
                    v.drain()
                    v.tensor_tensor(out=st_tst2[_p][:], in0=st_tst[_p][:],
                                    in1=st_ulo[_p][:], op=ALU.max)
                    return v.drain()
                thunks.append((0.4, lambda _f=tstmax_f: pl.step("vector", _f)))

                h, hf = u // 2, u % 2
                hpar = h % 2
                for j in range(8):
                    def fin_f(v, _p=par, _j=j):
                        return v.scalar_tensor_tensor(
                            out=Pbf[_j % 2][:], in0=ET[_p][:, _j * S:(_j + 1) * S],
                            scalar=st_tst2[_p][:, _j:_j + 1],
                            in1=ET[_p][:, _j * S:(_j + 1) * S],
                            op0=ALU.is_ge, op1=ALU.mult,
                            accum_out=st_zsel[_p][:, _j:_j + 1])
                    thunks.append((2.3, lambda _f=fin_f: pl.step(
                        "vector", _f, waits=[(s_ptt, s_ptt.n)],
                        incs=[(s_fin, 1)], war=True)))
                    for half in range(2):
                        def ptr(e, _j=j, _hf=half):
                            last = None
                            for t in range(8):
                                kc = _hf * 8 + t
                                last = e.transpose(ps_pt[:, t * 128:(t + 1) * 128],
                                                   Pbf[_j % 2][:, kc * 128:(kc + 1) * 128],
                                                   id_b[:])
                            return last
                        thunks.append((0.0, lambda _f=ptr: pl.step(
                            "tensor", _f,
                            waits=[(s_fin, s_fin.n), (s_ptev, s_ptev.n)],
                            incs=[(s_ptt, 1)])))

                        def ptev(v, _j=j, _hf=half):
                            dst = PT4[:].rearrange("p (kc q) -> p kc q", kc=16)
                            dst = dst[:, _hf * 8:(_hf + 1) * 8,
                                      (_j % 4) * 128:(_j % 4) * 128 + 128]
                            src = ps_pt[:].rearrange("p (kc q) -> p kc q", kc=8)
                            return v.tensor_copy(dst, src)
                        thunks.append((0.7, lambda _f=ptev: pl.step(
                            "vector", _f,
                            waits=[(s_ptt, s_ptt.n), (s_pv, s_pv.n)],
                            incs=[(s_ptev, 1)], war=True)))
                    if j % 4 == 3:
                        g = j // 4
                        cpar = g % 2

                        def pv(e, _hp=hpar, _c=cpar):
                            last = None
                            for kc in range(16):
                                last = e.matmul(ps_cx[_c][:],
                                                lhsT=Vt[_hp][:, kc * 128:(kc + 1) * 128],
                                                rhs=PT4[:, kc * 512:(kc + 1) * 512],
                                                start=(kc == 0), stop=(kc == 15))
                            return last
                        thunks.append((0.0, lambda _f=pv: pl.step(
                            "tensor", _f,
                            waits=[(s_ptev, s_ptev.n), (s_vtr, s_vtr.n), (s_ctx, s_ctx.n)],
                            incs=[(s_pv, 1)])))
                        def zi_f(v, _p=par, _g=g):
                            return v.reciprocal(st_zinv[_p][:, _g * 4:_g * 4 + 4],
                                                st_zsel[_p][:, _g * 4:_g * 4 + 4])
                        thunks.append((0.4, lambda _f=zi_f: pl.step(
                            "vector", _f, incs=[(s_zinv, 1)])))

                        def trc(e, _p=par, _g=g):
                            last = None
                            for t in range(4):
                                last = e.transpose(ps_aux[0:1, t * 128:(t + 1) * 128],
                                                   st_zinv[_p][:, _g * 4 + t:_g * 4 + t + 1],
                                                   id_f[:])
                            return last
                        thunks.append((0.0, lambda _f=trc: pl.step(
                            "tensor", _f,
                            waits=[(s_zbev, s_zbev.n), (s_zinv, s_zinv.n)],
                            incs=[(s_trc, 1)])))
                        thunks.append((0.4, lambda: pl.step(
                            "vector", lambda v: v.tensor_copy(zrow[:], ps_aux[0:1, :]),
                            waits=[(s_trc, s_trc.n)], incs=[(s_zrow, 1)], war=True)))

                        def zbm(e):
                            last = None
                            for t in range(4):
                                last = e.matmul(ps_aux[:, t * 128:(t + 1) * 128],
                                                lhsT=ones_row[:],
                                                rhs=zrow[0:1, t * 128:(t + 1) * 128],
                                                start=True, stop=True)
                            return last
                        thunks.append((0.0, lambda _f=zbm: pl.step(
                            "tensor", _f, waits=[(s_zrow, s_zrow.n)], incs=[(s_zbm, 1)])))
                        thunks.append((0.7, lambda: pl.step(
                            "vector", lambda v: v.tensor_copy(zb_sb[:], ps_aux[:]),
                            waits=[(s_zbm, s_zbm.n)], incs=[(s_zbev, 1)], war=True)))

                        def ctxev(v, _hf=hf, _g=g, _c=cpar):
                            v.drain()
                            dst = ctxh[:, (_hf * 8 + _g * 4) * 128:(_hf * 8 + _g * 4) * 128 + 512]
                            return v.tensor_tensor(out=dst, in0=ps_cx[_c][:], in1=zb_sb[:],
                                                   op=ALU.mult)
                        thunks.append((0.7, lambda _f=ctxev: pl.step(
                            "vector", _f,
                            waits=[(s_pv, s_pv.n), (s_zbev, s_zbev.n), (s_cst, s_cst.n)],
                            incs=[(s_ctx, 1)], war=True)))
                if hf == 1:
                    def cst_f(e, _h=h):
                        return e.dma_start(out=ctxT_d[_h * 128:(_h + 1) * 128, :],
                                           in_=ctxh[:])
                    thunks.append((0.0, lambda _f=cst_f: pl.step(
                        "sync", _f, waits=[(s_ctx, s_ctx.n)], incs=[(s_cst, 16)])))
                return thunks

            dma_head(0)
            emit_vtr(0)
            emit_stage_S(0)
            pending = []
            for u in range(NU):
                h, hf = u // 2, u % 2
                if hf == 0 and h + 1 < NH:
                    dma_head(h + 1)
                lo, hi, ch = emit_bisect(u, pending)
                if u + 1 < NU:
                    if (u + 1) % 2 == 0:
                        emit_vtr((u + 1) // 2)
                    emit_stage_S(u + 1)
                pending = make_tail_thunks(u, ch, hi)
            for _c, fn in pending:
                fn()
            pl.step("sync", lambda e: e.nop(), waits=[(s_cst, s_cst.n)])
            if debug and upto == 2:
                dbg = sem("p2_dbg")
                state_list = [st_zsum[1], st_tlo[0][1], st_tlo[1][1], st_thi[0][1],
                              st_thi[1][1], st_chi[0][1], st_chi[1][1], st_cnt[1],
                              st_u[1], st_ulo[1], st_uhi[1], st_tst[1], st_tst2[1],
                              st_zsel[1], st_zinv[1], st_tmid[1]]
                for si, stn in enumerate(state_list):
                    pl.step("sync", lambda e, _s=stn, _i=si: e.dma_start(
                        out=yT_d[0:128, _i * 8:(_i + 1) * 8], in_=_s[:]),
                        incs=[(dbg, 16)])
                for j in range(8):
                    pl.step("sync", lambda e, _j=j: e.dma_start(
                        out=yT_d[128 * (1 + _j):128 * (2 + _j), :],
                        in_=ET[1][:, _j * S:(_j + 1) * S]), incs=[(dbg, 16)])
                pl.step("sync", lambda e: e.nop(), waits=[(dbg, dbg.n)])
            block = ph.enter_context(nc.Block())
            pl.install(block)

        if upto < 3:
            return nc

        ln2scope = ExitStack()
        a2_row = ln2scope.enter_context(nc.sbuf_tensor("a2_row", [1, S], F32))
        b2_row = ln2scope.enter_context(nc.sbuf_tensor("b2_row", [1, S], F32))
        with ExitStack() as ph:
            pl = Plan()
            ctxsb = ph.enter_context(nc.sbuf_tensor("p3_ctx", [128, NMT * S], BF16))
            wb = ph.enter_context(nc.sbuf_tensor("p3_wb", [128, H], BF16))
            x0t = [ph.enter_context(nc.sbuf_tensor("p3_x0%d" % i, [128, S], F32)) for i in range(2)]
            x1t = [ph.enter_context(nc.sbuf_tensor("p3_x1%d" % i, [128, S], F32)) for i in range(2)]
            ps_o = ph.enter_context(nc.psum_tensor("p3_ps", [128, S], F32))
            ld = sem("p3_ld")
            ldx = sem("p3_ldx")
            ldc = sem("p3_ldc")
            mmd = sem("p3_mm")
            ev = sem("p3_ev")
            st = sem("p3_st")

            for c in range(NMT):
                pl.step("sync", lambda e, _c=c: e.dma_start(
                    out=ctxsb[:, _c * S:(_c + 1) * S],
                    in_=ctxT_d[_c * 128:(_c + 1) * 128, :]), incs=[(ldc, 16)])

            for mt in range(NMT):
                bi = mt % 2
                waits = [(mmd, mt)] if mt >= 1 else []
                pl.step("gpsimd", lambda e, _mt=mt: e.dma_start(
                    out=wb[:],
                    in_=wdev["wo"][_mt].rearrange("(c p) o -> c p o", p=128).transpose([1, 0, 2])),
                    waits=waits, incs=[(ld, 16)])
                waits = [(ev, mt - 1)] if mt >= 2 else []
                pl.step("sync", lambda e, _mt=mt, _b=bi: e.dma_start(
                    out=x0t[_b][:], in_=x0T[_mt * 128:(_mt + 1) * 128, :]),
                    waits=waits, incs=[(ldx, 16)])
                waits = [(ld, ld.n), (ldx, ldx.n), (ldc, ldc.n)]
                if mt >= 1:
                    waits.append((ev, mt))

                def mm(e):
                    last = None
                    for n in range(4):
                        for c in range(NMT):
                            last = e.matmul(
                                ps_o[:, n * 512:(n + 1) * 512],
                                lhsT=wb[:, c * 128:(c + 1) * 128],
                                rhs=ctxsb[:, c * S + n * 512: c * S + (n + 1) * 512],
                                start=(c == 0), stop=(c == NMT - 1))
                    return last
                pl.step("tensor", mm, waits=waits, incs=[(mmd, 1)])
                waits = [(mmd, mmd.n), (ld, ld.n)]
                if mt >= 2:
                    waits.append((st, 16 * (mt - 1)))
                pl.step("vector", lambda v, _b=bi, _mt=mt: v.scalar_tensor_tensor(
                    out=x1t[_b][:], in0=ps_o[:], scalar=bias_sb["bo"][:, _mt:_mt + 1],
                    in1=x0t[_b][:], op0=ALU.add, op1=ALU.add),
                    waits=waits, incs=[(ev, 1)])
                pl.step("sync", lambda e, _b=bi, _mt=mt: e.dma_start(
                    out=OUT[_mt * 128:(_mt + 1) * 128, :], in_=x1t[_b][:]),
                    waits=[(ev, ev.n)], incs=[(st, 16)])
            pl.step("sync", lambda e: e.nop(), waits=[(st, st.n)])

            block = ph.enter_context(nc.Block())
            pl.install(block)

        emit_ln_stats("p3b", OUT, a2_row, b2_row)

        if upto < 4:
            ln2scope.close()
            return nc
        with ExitStack() as ph:
            pl = Plan()
            a2b = ph.enter_context(nc.sbuf_tensor("p4_a2b", [128, S], F32))
            b2b = ph.enter_context(nc.sbuf_tensor("p4_b2b", [128, S], F32))
            gbuf = [ph.enter_context(nc.sbuf_tensor("p4_g%d" % i, [128, S], F32)) for i in range(4)]
            tgb = [ph.enter_context(nc.sbuf_tensor("p4_tg%d" % i, [128, S], BF16)) for i in range(4)]
            upb = [ph.enter_context(nc.sbuf_tensor("p4_upb%d" % i, [128, 2 * GM], BF16)) for i in range(2)]
            dnb = [ph.enter_context(nc.sbuf_tensor("p4_dnb%d" % i, [128, 8 * GD], BF16)) for i in range(2)]
            hid = ph.enter_context(nc.sbuf_tensor("p4_hid", [128, 8 * S], BF16))
            gsc = ph.enter_context(nc.sbuf_tensor("p4_gsc", [128, S], F32))
            ybuf = [ph.enter_context(nc.sbuf_tensor("p4_y%d" % i, [128, S], F32)) for i in range(2)]
            ps_h = [ph.enter_context(nc.psum_tensor("p4_ps%d" % i, [128, S], F32)) for i in range(2)]
            gld = sem("p4_gld")
            wld = sem("p4_wld")
            cast = sem("p4_cast")
            nrm = sem("p4_nrm")
            mmu = sem("p4_mmu")
            mmd2 = sem("p4_mmd")
            ysc = sem("p4_ysc")
            pe0 = sem("p4_pe0")
            hev = sem("p4_hev")

            def bc2(e):
                last = None
                for n in range(4):
                    last = e.matmul(ps_h[0][:, n * 512:(n + 1) * 512], lhsT=ones_row[:],
                                    rhs=a2_row[0:1, n * 512:(n + 1) * 512], start=True, stop=True)
                for n in range(4):
                    last = e.matmul(ps_h[1][:, n * 512:(n + 1) * 512], lhsT=ones_row[:],
                                    rhs=b2_row[0:1, n * 512:(n + 1) * 512], start=True, stop=True)
                return last
            pl.step("tensor", bc2, incs=[(pe0, 1)])
            pl.step("vector", lambda v: v.tensor_copy(a2b[:], ps_h[0][:]),
                    waits=[(pe0, 1)], incs=[(hev, 1)])
            pl.step("vector", lambda v: v.tensor_copy(b2b[:], ps_h[1][:]),
                    incs=[(hev, 1)])

            hps_user = 2
            for g in range(NG):
                wpar = g % 2
                waits = [(mmd2, 2 * (g - 1))] if g >= 2 else []
                pl.step("gpsimd", lambda e, _g=g, _p=wpar: e.dma_start(
                    out=upb[_p][:], in_=up_dev[_g]), waits=waits, incs=[(wld, 16)])
                pl.step("gpsimd", lambda e, _g=g, _p=wpar: e.dma_start(
                    out=dnb[_p][:], in_=down_dev[_g]), incs=[(wld, 16)])
                for c in range(2):
                    gi = (2 * g + c) % 4
                    waits = []
                    if 2 * g + c >= 4:
                        waits.append((nrm, 2 * g + c - 3))
                    pl.step("gpsimd", lambda e, _g=g, _c=c, _gi=gi: e.indirect_dma_start(
                        out=gbuf[_gi][:], out_offset=None, in_=OUT,
                        in_offset=bass.IndirectOffsetOnAxis(
                            perm_sb[:, 2 * _g + _c:2 * _g + _c + 1], 0)),
                        waits=waits, incs=[(gld, 16)])
                    waits = [(gld, gld.n), (hev, 2)]
                    if 2 * g + c >= 4:
                        waits.append((mmu, 8 * (g - 1)))

                    def norm(v, _gi=gi):
                        v.tensor_tensor(out=gsc[:], in0=gbuf[_gi][:], in1=a2b[:], op=ALU.mult)
                        v.drain()
                        return v.tensor_tensor(out=tgb[_gi][:], in0=gsc[:], in1=b2b[:],
                                               op=ALU.add)
                    pl.step("vector", norm, waits=waits, incs=[(nrm, 1)])
                for ms in range(8):
                    ppar = hps_user % 2
                    waits = [(wld, wld.n), (nrm, nrm.n), (hev, hps_user - 1)]

                    def mmup(e, _g=g, _ms=ms, _p=wpar, _pp=ppar):
                        last = None
                        for n in range(4):
                            for c in range(2):
                                last = e.matmul(
                                    ps_h[_pp][:, n * 512:(n + 1) * 512],
                                    lhsT=upb[_p][:, c * GM + _ms * 128: c * GM + _ms * 128 + 128],
                                    rhs=tgb[(2 * _g + c) % 4][:, n * 512:(n + 1) * 512],
                                    start=(c == 0), stop=(c == 1))
                        return last
                    pl.step("tensor", mmup, waits=waits, incs=[(mmu, 1)])
                    pl.step("scalar", lambda e, _ms=ms, _pp=ppar: e.activation(
                        out=hid[:, _ms * S:(_ms + 1) * S], in_=ps_h[_pp][:], func=AF.Gelu),
                        waits=[(mmu, mmu.n), (mmd2, mmd2.n)], incs=[(hev, 1)], war=True)
                    hps_user += 1
                for os in range(2):
                    ppar = hps_user % 2
                    ypar = (2 * g + os) % 2
                    waits = [(hev, hev.n), (hev, hps_user - 1)]

                    def mmdn(e, _os=os, _p=wpar, _pp=ppar):
                        last = None
                        for n in range(4):
                            for c in range(8):
                                last = e.matmul(
                                    ps_h[_pp][:, n * 512:(n + 1) * 512],
                                    lhsT=dnb[_p][:, c * GD + _os * 128: c * GD + _os * 128 + 128],
                                    rhs=hid[:, c * S + n * 512: c * S + (n + 1) * 512],
                                    start=(c == 0), stop=(c == 7))
                        return last
                    pl.step("tensor", mmdn, waits=waits, incs=[(mmd2, 1)])
                    waits = [(mmd2, mmd2.n)]
                    if 2 * g + os >= 2:
                        waits.append((ysc, 16 * (2 * g + os - 1)))
                    pl.step("vector", lambda v, _yp=ypar, _pp=ppar: v.tensor_copy(
                        ybuf[_yp][:], ps_h[_pp][:]),
                        waits=waits, incs=[(hev, 1)])
                    hps_user += 1
                    pl.step("gpsimd", lambda e, _g=g, _os=os, _yp=ypar: e.indirect_dma_start(
                        out=OUT, out_offset=bass.IndirectOffsetOnAxis(
                            perm_sb[:, 2 * _g + _os:2 * _g + _os + 1], 0),
                        in_=ybuf[_yp][:], in_offset=None, oob_is_err=True,
                        compute_op=ALU.add),
                        waits=[(hev, hev.n)], incs=[(ysc, 16)])
            pl.step("gpsimd", lambda e: e.nop(), waits=[(ysc, ysc.n)])
            block = ph.enter_context(nc.Block())
            pl.install(block)
        ln2scope.close()

    return nc


def prep_inputs(L0_attn_hidden_states, L0_attn_wq, L0_attn_bq, L0_attn_wk, L0_attn_bk,
                L0_attn_wv, L0_attn_bv, L0_attn_wo, L0_attn_bo, L0_ffn_up, L0_ffn_down,
                L0_ffn_perm):
    def slabs(w):
        wt = np.ascontiguousarray(np.asarray(w, np.float32).T)
        return np.ascontiguousarray(wt.reshape(H, NMT, 128).transpose(1, 0, 2))

    def bias_lay(bv):
        return np.ascontiguousarray(np.asarray(bv, np.float32).reshape(NMT, 128).T)

    shared = {
        "wq_dev": slabs(L0_attn_wq), "wk_dev": slabs(L0_attn_wk),
        "wv_dev": slabs(L0_attn_wv), "wo_dev": slabs(L0_attn_wo),
        "bq_dev": bias_lay(L0_attn_bq), "bk_dev": bias_lay(L0_attn_bk),
        "bv_dev": bias_lay(L0_attn_bv), "bo_dev": bias_lay(L0_attn_bo),
        "up_dev": np.ascontiguousarray(
            np.asarray(L0_ffn_up, np.float32).reshape(NG, 2, 128, GM)
            .transpose(0, 2, 1, 3).reshape(NG, 128, 2 * GM)),
        "down_dev": np.ascontiguousarray(
            np.asarray(L0_ffn_down, np.float32).reshape(NG, 8, 128, GD)
            .transpose(0, 2, 1, 3).reshape(NG, 128, 8 * GD)),
        "perm_dev": np.ascontiguousarray(
            np.asarray(L0_ffn_perm).astype(np.uint32).reshape(32, 128).T),
        "ident_b": np.eye(128, dtype=ml_dtypes.bfloat16),
        "ident_f": np.eye(128, dtype=np.float32),
    }
    x = np.asarray(L0_attn_hidden_states, np.float32)
    in_maps = []
    for ci in range(8):
        m = dict(shared)
        m["x0T"] = np.ascontiguousarray(x[ci].T)
        in_maps.append(m)
    return in_maps


def kernel(**inputs):
    in_maps = prep_inputs(**inputs)
    nc = bass.Bass("TRN2", target_bir_lowering=False)
    build_program(nc)
    res = run_bass_kernel_spmd(nc, in_maps, core_ids=list(range(8)))
    out = np.stack([np.asarray(r["OUT"]).T for r in res.results], axis=0)
    return np.ascontiguousarray(out.astype(np.float32))


